# version 22
# baseline (speedup 1.0000x reference)
import numpy as np
from contextlib import ExitStack

import concourse.bass as bass
import concourse.mybir as mybir
from concourse.bass_utils import run_bass_kernel_spmd

F32 = mybir.dt.float32
BF16 = mybir.dt.bfloat16
AF = mybir.ActivationFunctionType
ALU = mybir.AluOpType
AX = mybir.AxisListType

H = 2048
NG = 16
NFG = 64
CL = 256
NWARM = 2
NREAL = 4
WARM = NWARM * CL
TREAL = NREAL * CL
NCH = NWARM + NREAL
NCORE = 8


class Buf:
    __slots__ = ("w", "r", "name")

    def __init__(self, name=""):
        self.w = None
        self.r = []
        self.name = name


class Prog:
    def __init__(self):
        self.items = []
        self.cnt = {}
        self.hw = {}
        self.waited = set()

    def op(self, eng, fn, reads=(), writes=(), dma=False, sem=None):
        semkey = sem if sem else (eng + "d" if dma else eng)
        amt = 16 if dma else 1
        need = {}

        def add(tok):
            if tok is not None and need.get(tok[0], -1) < tok[1]:
                need[tok[0]] = tok[1]

        for b in reads:
            add(b.w)
        for b in writes:
            add(b.w)
            for t in b.r:
                add(t)
        v = self.cnt.get(semkey, 0) + amt
        self.cnt[semkey] = v
        tok = (semkey, v)
        waits = []
        for sk, nv in sorted(need.items()):
            if eng == "pe" and sk == "pe":
                continue
            if self.hw.get((eng, sk), 0) >= nv:
                continue
            waits.append((sk, nv))
            self.hw[(eng, sk)] = nv
            self.waited.add((sk, nv))
        self.items.append([eng, fn, waits, semkey, amt, v])
        for b in reads:
            b.r.append(tok)
        for b in writes:
            b.w = tok
            b.r = []
        return tok

    def finalize(self):
        keep, newv, cnt = {}, {}, {}
        for it in self.items:
            _, _, _, semkey, amt, v = it
            tk = (semkey, v)
            if amt == 16 or tk in self.waited:
                cnt[semkey] = cnt.get(semkey, 0) + amt
                keep[tk] = True
                newv[tk] = cnt[semkey]
            else:
                keep[tk] = False
        for it in self.items:
            it[2] = [(sk, newv[(sk, v)]) for sk, v in it[2]]
            it[5] = newv[(it[3], it[5])] if keep[(it[3], it[5])] else None
        self.final_counts = cnt

    def emit(self, engname, engobj, sems):
        for eng, fn, waits, semkey, amt, newval in self.items:
            if eng != engname:
                continue
            for sk, v in waits:
                engobj.wait_ge(sems[sk], v)
            inst = fn(engobj)
            if newval is not None:
                inst.then_inc(sems[semkey], amt)


STAT = {"mu": 0, "v": 1, "t": 2, "t2": 3,
        "fA": 4, "fAC": 5, "fr": 6, "frC": 7,
        "p3r": 8, "p3rC": 9, "p4r": 8, "p4rC": 9,
        "p5r": 8, "p5rC": 9}


def build(nb=4, do_fold=True):
    TL = NCH * CL
    nc = bass.Bass("TRN2", target_bir_lowering=False)

    def din(name, shape, dtype=F32):
        return nc.dram_tensor(name, shape, dtype, kind="ExternalInput").ap()

    XS = din("xs", [nb, NG, 128, TL])
    WKT = din("wkT", [NG, 128, NG, 128])
    WVT = din("wvT", [NG, 128, NG, 128])
    WRT = din("wrT", [NG, 128, NG, 128])
    WOT = din("woT", [NG, 128, NG, 128])
    W1T = din("w1T", [NFG, 128, NG, 128])
    W3T = din("w3T", [NFG, 128, NG, 128])
    W2A = din("w2A", [64, 128, H])
    LNW = din("lnw", [128, NG])
    LNB = din("lnb", [128, NG])
    MKc = din("mk", [128, NG])
    MVc = din("mv", [128, NG])
    MRc = din("mr", [128, NG])
    TDc = din("td", [128, NG])
    UUc = din("uu", [128, NG])
    KM = din("kmask", [128, 1])
    GM = din("gmask", [128, 1])
    H64 = din("h64", [64, 64])
    H128S = din("h128s", [128, 128])
    ONESC = din("onesc", [128, 1])
    ONESR = din("onesr", [1, 128])

    OUT = nc.dram_tensor("out", [nb, NG, 128, TREAL], F32,
                         kind="ExternalOutput").ap()
    WKS = nc.dram_tensor("wkS", [NG, 128, NG, 128], BF16).ap()
    WVS = nc.dram_tensor("wvS", [NG, 128, NG, 128], BF16).ap()
    WRS = nc.dram_tensor("wrS", [NG, 128, NG, 128], BF16).ap()
    WOS = nc.dram_tensor("woS", [NG, 128, NG, 128], BF16).ap()
    W1S = nc.dram_tensor("w1S", [NFG, 128, NG, 128], BF16).ap()
    W3S = nc.dram_tensor("w3S", [NFG, 128, NG, 128], BF16).ap()
    W2AB = nc.dram_tensor("w2Ab", [64, 128, H], BF16).ap()
    SCRA = nc.dram_tensor("scrA", [64, 128, H], BF16).ap()
    W2S = nc.dram_tensor("w2S", [NG, 128, NFG, 128], BF16).ap()

    P = Prog()

    with ExitStack() as es:
        sb_base = nc._sbuf_addr_for_side("left")
        sb_top = sb_base + nc.sbuf_bytes_remaining
        cur = [sb_base]

        def A(name, shape, dtype):
            per = int(np.prod(shape[1:])) * mybir.dt.size(dtype)
            off = (cur[0] + 63) & ~63
            cur[0] = off + per
            assert off + per <= sb_top, f"sbuf overflow {name} {off + per}"
            return nc.alloc_sbuf_tensor_at(name, shape, dtype,
                                           offset=off).ap()

        x_t = A("x_t", [128, NG * CL], F32)
        sqb = A("sqb", [128, 2 * CL], BF16)
        strow = [A(f"strow{i}", [1, CL], F32) for i in range(10)]
        hirow = A("hirow", [1, CL], BF16)
        lorow = A("lorow", [1, CL], BF16)
        wbuf = [A(f"wbuf{i}", [128, NG * 128], BF16) for i in range(4)]
        outst = A("outst", [128, 8 * CL], F32)
        x_t2 = A("x_t2", [128, NG * CL], F32)
        tmp32 = [A(f"tmp32_{i}", [128, CL], F32) for i in range(2)]
        dmix = A("dmix", [128, CL], BF16)
        lnw_t = A("lnw_t", [128, NG], F32)
        lnb_t = A("lnb_t", [128, NG], F32)
        mk_t = A("mk_t", [128, NG], F32)
        mv_t = A("mv_t", [128, NG], F32)
        mr_t = A("mr_t", [128, NG], F32)
        td_t = A("td_t", [128, NG], F32)
        uu_t = A("uu_t", [128, NG], F32)
        lam_t = A("lam_t", [128, NG], F32)
        etmp_t = A("etmp_t", [128, NG], F32)
        km_t = A("km_t", [128, 1], F32)
        gm_t = A("gm_t", [128, 1], F32)
        onesc_b = A("onesc_b", [128, 1], BF16)
        onesr_b = A("onesr_b", [1, 128], BF16)
        h64_b = A("h64_b", [64, 64], BF16)
        h128s_b = A("h128s_b", [128, 128], BF16)
        carr_m = A("carr_m", [128, 4 * NG], F32)
        carr_D = A("carr_D", [128, 4 * NG], F32)
        carr_E = A("carr_E", [128, 4 * NG], F32)
        scol = A("scol", [128, 8], F32)
        epsc = A("epsc", [1, 8], F32)
        guard_t = A("guard_t", [128, 4 * NG], BF16)
        xn_t = A("xn_t", [128, NG * (CL + 1)], BF16)
        xk_t = A("xk_t", [128, NG * CL], BF16)
        xv_t = A("xv_t", [128, NG * CL], BF16)
        xr_t = A("xr_t", [128, NG * CL], BF16)
        kq_t = A("kq_t", [128, 2 * CL], F32)
        vq_t = A("vq_t", [128, 2 * CL], F32)
        p_t = A("p_t", [128, CL], F32)
        pv_t = A("pv_t", [128, CL], F32)
        bon_t = A("bon_t", [128, CL], F32)
        D_t = A("D_t", [128, CL + 1], F32)
        E_t = A("E_t", [128, CL + 1], F32)
        r_t = A("r_t", [128, NG * CL], BF16)
        ao_t = A("ao_t", [128, NG * CL], BF16)
        xbf_t = ao_t
        s35_t = A("s35_t", [128, NG * CL], BF16)
        x2b_t = A("x2b_t", [128, NG * CL], BF16)
        fin_t = A("fin_t", [128, NG * CL], BF16)
        fout_t = fin_t
        hid_t = A("hid_t", [128, NFG * CL], BF16)
        hids = A("hids", [128, 2 * CL], BF16)
        fA = A("fA", [128, 2048], BF16)
        fB = A("fB", [128, NG * 128], BF16)
        fE = A("fE", [128, 2 * 2048], BF16)

        psf = [es.enter_context(nc.psum_tensor(f"ps{i}", [128, 512], F32))
               for i in range(8)]
        ps = [t[:, 0:CL] for t in psf]
        PS_POOL = ps[0:4]
        rot = [0]

        semkeys = (["pe", "dv", "sc", "gp", "gpd", "x0", "x1", "ost",
                    "fa", "fb", "fs0", "fs1", "ws"] +
                   [f"w{i}" for i in range(4)] +
                   [f"c{i}" for i in range(8)])
        sems = {k: es.enter_context(nc.semaphore(f"s_{k}")) for k in semkeys}
        block = es.enter_context(nc.Block())

        def bufs(pref, n):
            return [Buf(f"{pref}{i}") for i in range(n)]

        b_x2 = [bufs("xA", NG), bufs("xB", NG)]

        b_xn = bufs("xn", NG)
        b_xk = bufs("xk", NG)
        b_xv = bufs("xv", NG)
        b_xr = bufs("xr", NG)
        b_kq = bufs("kq", 2)
        b_vq = bufs("vq", 2)
        b_p, b_pv, b_bon = Buf("p"), Buf("pv"), Buf("bon")
        b_D, b_E = Buf("D"), Buf("E")
        b_r = bufs("r", NG)
        b_ao = bufs("ao", NG)
        b_xbf = b_ao
        b_s35 = bufs("s35", NG)
        b_x2b = bufs("x2b", NG)
        b_fin = bufs("fin", NG)
        b_fout = b_fin
        b_hid = bufs("hid", NFG)
        b_hids = bufs("hids", 2)
        b_sq = bufs("sq", 2)
        b_wbuf = bufs("wbuf", 4)
        b_out = bufs("outst", 8)
        b_tmp = bufs("tmp32_", 2)
        b_dmix = Buf("dmix")
        b_ps = [Buf(f"ps{i}") for i in range(8)]
        b_row = [Buf(f"row{i}") for i in range(10)]
        b_hi, b_lo = Buf("hirow"), Buf("lorow")
        b_carr = bufs("carr", 4 * NG)
        b_scol = Buf("scol")
        b_consts = Buf("consts")
        b_guard = bufs("guard", 4)
        b_fA = Buf("fA")
        b_fB = Buf("fB")
        b_fE = bufs("fE", 4)
        b_scrA = Buf("scrA")
        b_scrA2 = Buf("scrA2")
        b_wdram = {n: Buf("wd_" + n) for n in
                   ["wk", "wv", "wr", "wo", "w1", "w3", "w2ab", "w2s"]}
        b_io = Buf("io")

        def pair(peek=False):
            i = rot[0] % 2
            if not peek:
                rot[0] += 1
            return (ps[4 + i * 2], ps[5 + i * 2],
                    b_ps[4 + i * 2], b_ps[5 + i * 2])

        def row(i):
            return strow[i][:]

        def dv(fn, reads, writes):
            return P.op("dv", fn, reads, writes)

        def sc(fn, reads, writes):
            return P.op("sc", fn, reads, writes)

        def pe(fn, reads, writes):
            return P.op("pe", fn, reads, writes)

        def gp(fn, reads, writes):
            return P.op("gp", fn, reads, writes)

        def gpd(fn, reads, writes, sem=None):
            return P.op("gp", fn, reads, writes, dma=True, sem=sem)

        def syd(fn, reads, writes, sem):
            return P.op("sy", fn, reads, writes, dma=True, sem=sem)

        for src, dst in ((LNW, lnw_t), (LNB, lnb_t), (MKc, mk_t),
                         (MVc, mv_t), (MRc, mr_t), (TDc, td_t), (UUc, uu_t),
                         (KM, km_t), (GM, gm_t), (H64, h64_b),
                         (H128S, h128s_b), (ONESC, onesc_b),
                         (ONESR, onesr_b)):
            gpd(lambda e, s=src, d=dst: e.dma_start(out=d[:], in_=s),
                [], [b_consts])
        for i, v in enumerate((1e-10, 1e-5, 1.00001, -0.5)):
            dv(lambda e, i=i, v=v: e.memset(epsc[0:1, i:i + 1], v),
               [], [b_consts])
        sc(lambda e: e.activation(etmp_t[:], td_t[:], AF.Exp),
           [b_consts], [b_consts])
        sc(lambda e: e.activation(lam_t[:], etmp_t[:], AF.Exp, scale=-1.0),
           [b_consts], [b_consts])

        castn = [0]

        def cast_w(src, dst, bname):
            gpd(lambda e, s=src, d=dst: e.dma_start(out=d, in_=s),
                [], [b_wdram[bname]], sem=f"c{castn[0]}")
            castn[0] += 1

        for csrc, cdst, cnm in [(WKT, WKS, "wk"), (WVT, WVS, "wv"),
                                (W2A, W2AB, "w2ab"), (WRT, WRS, "wr"),
                                (WOT, WOS, "wo"), (W1T, W1S, "w1"),
                                (W3T, W3S, "w3")]:
            cast_w(csrc, cdst, cnm)

        def bcast(ri, psd, psb):
            rap = row(ri)
            rb = b_row[ri]
            dv(lambda e: e.tensor_copy(hirow[:], rap), [rb], [b_hi])
            dv(lambda e: e.tensor_sub(lorow[:], rap, hirow[:]),
               [rb, b_hi], [b_lo])
            pe(lambda e: e.matmul(psd[:], onesr_b[:], hirow[:],
                                  start=True, stop=False),
               [b_consts, b_hi], [psb])
            pe(lambda e: e.matmul(psd[:], onesr_b[:], lorow[:],
                                  start=False, stop=True),
               [b_consts, b_lo], [psb])

        def stats(aps, bbs, pA, pB, bA, bB):
            for g in range(NG):
                sq = sqb[:, (g % 2) * CL:(g % 2) * CL + CL]
                sc(lambda e, g=g, sq=sq: e.activation(sq, aps[g], AF.Square),
                   [bbs[g]], [b_sq[g % 2]])
                pe(lambda e, g=g, pA=pA: e.matmul(
                    pA[0:1, :], onesc_b[:], aps[g],
                    start=(g == 0), stop=(g == NG - 1)),
                   [b_consts, bbs[g]],
                   [bA] if g in (0, NG - 1) else [])
                pe(lambda e, g=g, sq=sq, pB=pB: e.matmul(
                    pB[0:1, :], onesc_b[:], sq,
                    start=(g == 0), stop=(g == NG - 1)),
                   [b_consts, b_sq[g % 2]],
                   [bB] if g in (0, NG - 1) else [])

        def stat_math(pref, inner, pA, pB, bA, bB):
            iu, iv = STAT["mu"], STAT["v"]
            it, it2 = STAT["t"], STAT["t2"]
            dv(lambda e, pA=pA: e.tensor_copy(row(iu), pA[0:1, :]),
               [bA], [b_row[iu]])
            dv(lambda e: e.tensor_mul(row(it), row(iu), row(iu)),
               [b_row[iu]], [b_row[it]])
            dv(lambda e, pB=pB: e.tensor_sub(row(iv), pB[0:1, :], row(it)),
               [bB, b_row[it]], [b_row[iv]])
            specs = ([(pref + "A", True)] if inner else []) + \
                [(pref + "r", False)]
            for nm, isA in specs:
                io, ic = STAT[nm], STAT[nm + "C"]
                bia = epsc[0:1, 0:1] if isA else epsc[0:1, 1:2]
                sca = epsc[0:1, 2:3] if isA else 1.0
                sc(lambda e, bia=bia, sca=sca: e.activation(
                    row(it2), row(iv), AF.Ln, bias=bia, scale=sca),
                   [b_row[iv], b_consts], [b_row[it2]])
                sc(lambda e, io=io: e.activation(
                    row(io), row(it2), AF.Exp, scale=epsc[0:1, 3:4]),
                   [b_row[it2], b_consts], [b_row[io]])
                dv(lambda e, io=io, ic=ic: e.tensor_mul(row(ic), row(iu),
                                                        row(io)),
                   [b_row[iu], b_row[io]], [b_row[ic]])

        def stat_round(aps, bbs, pref, inner, bnames):
            pA, pB, bA, bB = pair()
            stats(aps, bbs, pA, pB, bA, bB)
            stat_math(pref, inner, pA, pB, bA, bB)
            bcast(STAT[bnames[0]], pA, bA)
            bcast(STAT[bnames[1]], pB, bB)
            return pA, pB, bA, bB

        def mm_group(psd, psb, w_ap, wslot, rhs_t, rhs_bufs, n=NG):
            for kk in range(n):
                pe(lambda e, kk=kk: e.matmul(
                    psd[:], w_ap[:, kk * 128:(kk + 1) * 128],
                    rhs_t[:, kk * CL:(kk + 1) * CL],
                    start=(kk == 0), stop=(kk == n - 1)),
                   [wslot, rhs_bufs[kk]],
                   [psb] if kk in (0, n - 1) else [])

        def wload(sl, src, src_buf):
            eng = "sy" if sl < 2 else "sc"
            P.op(eng, lambda e: e.dma_start(
                out=wbuf[sl][:].rearrange("p (k m) -> p k m", m=128),
                in_=src),
                [src_buf], [b_wbuf[sl]], dma=True, sem=f"w{sl}")

        def emit_fold():
            for pp in range(128):
                sl = pp % 2
                fa = fA[0:64, :]
                fe = fE[0:64, sl * 2048:(sl + 1) * 2048]
                syd(lambda e, pp=pp, fa=fa: e.dma_start(
                    out=fa, in_=W2AB[:, pp, :]),
                    [b_wdram["w2ab"]], [b_fA], sem="fa")
                for d8 in range(8):
                    pst, pb = ps[d8 % 8], b_ps[d8 % 8]
                    feb = b_fE[sl * 2 + d8 % 2]
                    pe(lambda e, fa=fa, d8=d8, pst=pst: e.matmul(
                        pst[0:64, :], h64_b[:],
                        fa[:, d8 * 256:d8 * 256 + 256],
                        start=True, stop=True),
                       [b_consts, b_fA], [pb])
                    if d8 % 2:
                        sc(lambda e, fe=fe, d8=d8, pst=pst: e.copy(
                            fe[:, d8 * 256 - 2048 * 0:(d8 + 1) * 256]
                            if False else fe[:, d8 * 256:(d8 + 1) * 256],
                            pst[0:64, :]), [pb], [feb])
                    else:
                        dv(lambda e, fe=fe, d8=d8, pst=pst: e.tensor_copy(
                            fe[:, d8 * 256:(d8 + 1) * 256],
                            pst[0:64, :]), [pb], [feb])
                syd(lambda e, pp=pp, fe=fe, sl=sl: e.dma_start(
                    out=SCRA[:, pp, :], in_=fe),
                    [b_fE[sl * 2], b_fE[sl * 2 + 1]],
                    [b_scrA if sl == 0 else b_scrA2],
                    sem=f"fs{sl}")
            for g in range(64):
                syd(lambda e, g=g: e.dma_start(out=fB[:], in_=SCRA[g]),
                    [b_scrA, b_scrA2], [b_fB], sem="fb")
                for d8 in range(8):
                    sl = d8 % 4
                    pst, pb = ps[sl], b_ps[sl]
                    pe(lambda e, d8=d8, pst=pst: e.matmul(
                        pst[:], h128s_b[:],
                        fB[:, d8 * 256:d8 * 256 + 256],
                        start=True, stop=True),
                       [b_consts, b_fB], [pb])
                    fe = fE[:, d8 * 256:(d8 + 1) * 256]
                    sc(lambda e, fe=fe, pst=pst: e.copy(fe, pst[:]),
                       [pb], [b_fE[sl]])

                    def store(e, g=g, d8=d8, fe=fe):
                        dstv = W2S.rearrange(
                            "ogd p kf m -> p kf ogd m")[:, g,
                                                        d8 * 2:(d8 + 1) * 2]
                        srcv = fe.rearrange("p (ogd m) -> p ogd m", m=128)
                        return e.dma_start(out=dstv, in_=srcv)
                    syd(store, [b_fE[sl]], [b_wdram["w2s"]], sem="ws")

        def emit_xload(b, s, xp):
            t0 = s * CL
            xt = x_t if xp == 0 else x_t2
            tok = None
            for g in range(NG):
                tok = P.op("sc", lambda e, b=b, g=g, t0=t0, xt=xt:
                           e.dma_start(out=xt[:, g * CL:(g + 1) * CL],
                                       in_=XS[b, g, :, t0:t0 + CL]),
                           [], [b_x2[xp][g]], dma=True, sem=f"x{xp}")
            for g in range(NG):
                b_x2[xp][g].w = tok

        def front(b, s, xp):
            xt = x_t if xp == 0 else x_t2
            for g in range(NG):
                gp(lambda e, g=g, xt=xt: e.tensor_copy(
                    xbf_t[:, g * CL:(g + 1) * CL],
                    xt[:, g * CL:(g + 1) * CL]), [b_x2[xp][g]], [b_xbf[g]])
            xbf_aps = [xbf_t[:, g * CL:(g + 1) * CL] for g in range(NG)]
            return stat_round(xbf_aps, b_xbf, "f", True, ["fA", "fAC"])

        def xnmix(b, s, xp, fpair):
            warm = s < NWARM
            xt = x_t if xp == 0 else x_t2
            b_x = b_x2[xp]
            pA, pB, bA, bB = fpair
            cb = b * NG
            bslc = slice(cb, cb + NG)
            if s == 0:
                dv(lambda e, bslc=bslc: e.memset(carr_m[:, bslc], -1e30),
                   [], b_carr[cb:cb + NG])
                dv(lambda e, bslc=bslc: e.memset(carr_D[:, bslc], 0.0),
                   [], b_carr[cb:cb + NG])
                dv(lambda e, bslc=bslc: e.memset(carr_E[:, bslc], 0.0),
                   [], b_carr[cb:cb + NG])
                dv(lambda e, bslc=bslc: e.memset(guard_t[:, bslc], 0.0),
                   [], [b_guard[b]])
            if s == NWARM:
                dv(lambda e, bslc=bslc: e.tensor_scalar_mul(
                    xn_t[:].rearrange("p (g c) -> p g c",
                                      c=CL + 1)[:, :, 0],
                    guard_t[:, bslc], gm_t[:]),
                   [b_guard[b], b_consts], b_xn)
            else:
                dv(lambda e, bslc=bslc: e.tensor_copy(
                    xn_t[:].rearrange("p (g c) -> p g c",
                                      c=CL + 1)[:, :, 0],
                    guard_t[:, bslc]), [b_guard[b]], b_xn)
            for g in range(NG):
                xg0 = xt[:, g * CL:(g + 1) * CL]
                xng = xn_t[:, g * (CL + 1) + 1:(g + 1) * (CL + 1)]
                sxg = xn_t[:, g * (CL + 1):g * (CL + 1) + CL]
                tm = tmp32[g % 2]
                dv(lambda e, xg0=xg0, tm=tm, pA=pA: e.tensor_mul(
                    tm[:], xg0, pA[:]), [b_x[g], bA], [b_tmp[g % 2]])
                dv(lambda e, tm=tm, pB=pB: e.tensor_sub(tm[:], tm[:],
                                                        pB[:]),
                   [b_tmp[g % 2], bB], [b_tmp[g % 2]])
                dv(lambda e, g=g, xng=xng, tm=tm: e.tensor_scalar(
                    xng, tm[:], lnw_t[:, g:g + 1], lnb_t[:, g:g + 1],
                    ALU.mult, ALU.add),
                   [b_tmp[g % 2], b_consts], [b_xn[g]])
                dv(lambda e, xng=xng, sxg=sxg: e.tensor_sub(dmix[:], xng,
                                                            sxg),
                   [b_xn[g]], [b_dmix])
                mixes = [(mk_t, xk_t, b_xk), (mv_t, xv_t, b_xv)]
                if not warm:
                    mixes.append((mr_t, xr_t, b_xr))
                for mt, dstt, bb2 in mixes:
                    dv(lambda e, g=g, mt=mt, dstt=dstt, sxg=sxg:
                       e.scalar_tensor_tensor(
                           dstt[:, g * CL:(g + 1) * CL], dmix[:],
                           mt[:, g:g + 1], sxg, ALU.mult, ALU.add),
                       [b_dmix, b_xn[g], b_consts], [bb2[g]])
            dv(lambda e, bslc=bslc: e.tensor_copy(
                guard_t[:, bslc],
                xn_t[:].rearrange("p (g c) -> p g c", c=CL + 1)[:, :, CL]),
               b_xn, [b_guard[b]])

        def projscan(b, s):
            warm = s < NWARM
            cb = b * NG

            def proj_emit(hg):
                par = hg % 2
                sk, sv = (hg * 2) % 4, (hg * 2 + 1) % 4
                wload(sk, WKS[hg], b_wdram["wk"])
                wload(sv, WVS[hg], b_wdram["wv"])
                psk, bpsk = PS_POOL[2 * par], b_ps[2 * par]
                psv, bpsv = PS_POOL[2 * par + 1], b_ps[2 * par + 1]
                mm_group(psk, bpsk, wbuf[sk], b_wbuf[sk], xk_t, b_xk)
                mm_group(psv, bpsv, wbuf[sv], b_wbuf[sv], xv_t, b_xv)
                kq = kq_t[:, par * CL:par * CL + CL]
                vq = vq_t[:, par * CL:par * CL + CL]
                sc(lambda e, kq=kq, psk=psk: e.copy(kq, psk[:]),
                   [bpsk], [b_kq[par]])
                sc(lambda e, vq=vq, psv=psv: e.copy(vq, psv[:]),
                   [bpsv], [b_vq[par]])
                if not warm:
                    wload(sk, WRS[hg], b_wdram["wr"])
                    rA, rB, rbA, rbB = pair(peek=True)
                    psr, bpsr = (rA, rbA) if par == 0 else (rB, rbB)
                    mm_group(psr, bpsr, wbuf[sk], b_wbuf[sk], xr_t, b_xr)
                    rg = r_t[:, hg * CL:(hg + 1) * CL]
                    sc(lambda e, rg=rg, psr=psr: e.activation(
                        rg, psr[:], AF.Sigmoid), [bpsr], [b_r[hg]])

            def scan_emit(hg):
                par = hg % 2
                kq = kq_t[:, par * CL:par * CL + CL]
                vq = vq_t[:, par * CL:par * CL + CL]
                lm, mc, negm = scol[:, 0:1], scol[:, 1:2], scol[:, 2:3]
                ub, cs = scol[:, 3:4], scol[:, 4:5]
                mp = carr_m[:, cb + hg:cb + hg + 1]
                Dc = carr_D[:, cb + hg:cb + hg + 1]
                Ec = carr_E[:, cb + hg:cb + hg + 1]
                bca = b_carr[cb + hg]
                dv(lambda e, kq=kq: e.reduce_max(lm, kq, axis=AX.X),
                   [b_kq[par]], [b_scol])
                if warm:
                    dv(lambda e: e.tensor_add(lm, lm, km_t[:]),
                       [b_scol, b_consts], [b_scol])
                dv(lambda e, mp=mp: e.tensor_max(mc, lm, mp),
                   [b_scol, bca], [b_scol])
                dv(lambda e: e.tensor_scalar_mul(negm, mc, -1.0),
                   [b_scol], [b_scol])
                pb = scol[:, 5:6]
                if warm:
                    dv(lambda e: e.tensor_add(pb, negm, km_t[:]),
                       [b_scol, b_consts], [b_scol])
                else:
                    pb = negm
                sc(lambda e, kq=kq, pb=pb: e.activation(p_t[:], kq, AF.Exp,
                                                        bias=pb),
                   [b_kq[par], b_scol], [b_p])
                dv(lambda e, vq=vq: e.tensor_mul(pv_t[:], p_t[:], vq),
                   [b_p, b_vq[par]], [b_pv])
                sc(lambda e, mp=mp: e.activation(cs, mp, AF.Exp, bias=negm),
                   [bca, b_scol], [b_scol])
                dv(lambda e, Dc=Dc: e.tensor_mul(D_t[:, 0:1], Dc, cs),
                   [bca, b_scol], [b_D])
                dv(lambda e, Ec=Ec: e.tensor_mul(E_t[:, 0:1], Ec, cs),
                   [bca, b_scol], [b_E])
                lamb = lam_t[:, hg:hg + 1].broadcast_to([128, CL])
                dv(lambda e, lamb=lamb: e.tensor_tensor_scan(
                    D_t[:, 1:CL + 1], lamb, pv_t[:], D_t[:, 0:1],
                    ALU.mult, ALU.add),
                   [b_consts, b_pv, b_D], [b_D])
                dv(lambda e, lamb=lamb: e.tensor_tensor_scan(
                    E_t[:, 1:CL + 1], lamb, p_t[:], E_t[:, 0:1],
                    ALU.mult, ALU.add),
                   [b_consts, b_p, b_E], [b_E])
                dv(lambda e, Dc=Dc: e.tensor_copy(Dc, D_t[:, CL:CL + 1]),
                   [b_D], [bca])
                dv(lambda e, Ec=Ec: e.tensor_copy(Ec, E_t[:, CL:CL + 1]),
                   [b_E], [bca])
                dv(lambda e, mp=mp: e.tensor_copy(mp, mc),
                   [b_scol], [bca])
                if not warm:
                    dv(lambda e, hg=hg: e.tensor_add(
                        ub, uu_t[:, hg:hg + 1], negm),
                       [b_consts, b_scol], [b_scol])
                    sc(lambda e, kq=kq: e.activation(bon_t[:], kq, AF.Exp,
                                                     bias=ub),
                       [b_kq[par], b_scol], [b_bon])
                    dv(lambda e, vq=vq: e.tensor_mul(pv_t[:], bon_t[:], vq),
                       [b_bon, b_vq[par], b_pv], [b_pv])
                    dv(lambda e: e.tensor_add(pv_t[:], pv_t[:],
                                              D_t[:, 0:CL]),
                       [b_pv, b_D], [b_pv])
                    dv(lambda e: e.tensor_add(p_t[:], bon_t[:],
                                              E_t[:, 0:CL]),
                       [b_bon, b_E, b_p], [b_p])
                    sc(lambda e: e.activation(bon_t[:], p_t[:], AF.Ln),
                       [b_p], [b_bon])
                    sc(lambda e: e.activation(bon_t[:], bon_t[:], AF.Exp,
                                              scale=-1.0),
                       [b_bon], [b_bon])
                    dv(lambda e: e.tensor_mul(p_t[:], pv_t[:], bon_t[:]),
                       [b_pv, b_bon, b_p], [b_p])
                    rg = r_t[:, hg * CL:(hg + 1) * CL]
                    dv(lambda e, rg=rg: e.tensor_mul(rg, rg, p_t[:]),
                       [b_r[hg], b_p], [b_r[hg]])

            for hg in range(NG):
                proj_emit(hg)
                if hg > 0:
                    scan_emit(hg - 1)
            scan_emit(NG - 1)

        def body(b, s, xp, nxtp, nxt):
            xt = x_t if xp == 0 else x_t2
            b_x = b_x2[xp]
            sr = s - NWARM
            pA1, pB1, bA1, bB1 = pair()
            bcast(STAT["fr"], pA1, bA1)
            bcast(STAT["frC"], pB1, bB1)
            pA3, pB3, bA3, bB3 = pair()
            for og in range(NG):
                sl = og % 4
                wload(sl, WOS[og], b_wdram["wo"])
                pso, bpso = PS_POOL[sl], b_ps[sl]
                mm_group(pso, bpso, wbuf[sl], b_wbuf[sl], r_t, b_r)
                xg = xt[:, og * CL:(og + 1) * CL]
                aog = ao_t[:, og * CL:(og + 1) * CL]
                tm = tmp32[og % 2]
                dv(lambda e, xg=xg, tm=tm, pA1=pA1: e.tensor_mul(
                    tm[:], xg, pA1[:]), [b_x[og], bA1], [b_tmp[og % 2]])
                dv(lambda e, tm=tm, pso=pso: e.tensor_add(tm[:], tm[:],
                                                          pso[:]),
                   [b_tmp[og % 2], bpso], [b_tmp[og % 2]])
                dv(lambda e, tm=tm, aog=aog, pB1=pB1: e.tensor_sub(
                    aog, tm[:], pB1[:]), [b_tmp[og % 2], bB1], [b_ao[og]])
                sq = sqb[:, (og % 2) * CL:(og % 2) * CL + CL]
                sc(lambda e, sq=sq, aog=aog: e.activation(sq, aog,
                                                          AF.Square),
                   [b_ao[og]], [b_sq[og % 2]])
                pe(lambda e, og=og, aog=aog, pA3=pA3: e.matmul(
                    pA3[0:1, :], onesc_b[:], aog,
                    start=(og == 0), stop=(og == NG - 1)),
                   [b_consts, b_ao[og]],
                   [bA3] if og in (0, NG - 1) else [])
                pe(lambda e, og=og, sq=sq, pB3=pB3: e.matmul(
                    pB3[0:1, :], onesc_b[:], sq,
                    start=(og == 0), stop=(og == NG - 1)),
                   [b_consts, b_sq[og % 2]],
                   [bB3] if og in (0, NG - 1) else [])
            stat_math("p3", False, pA3, pB3, bA3, bB3)
            bcast(STAT["p3r"], pA3, bA3)
            bcast(STAT["p3rC"], pB3, bB3)
            for og in range(NG):
                aog = ao_t[:, og * CL:(og + 1) * CL]
                sg = s35_t[:, og * CL:(og + 1) * CL]
                xg = xt[:, og * CL:(og + 1) * CL]
                x2g = x2b_t[:, og * CL:(og + 1) * CL]
                tm = tmp32[og % 2]
                dv(lambda e, aog=aog, tm=tm, pA3=pA3: e.tensor_mul(
                    tm[:], aog, pA3[:]), [b_ao[og], bA3], [b_tmp[og % 2]])
                dv(lambda e, tm=tm, sg=sg, pB3=pB3: e.tensor_sub(
                    sg, tm[:], pB3[:]), [b_tmp[og % 2], bB3], [b_s35[og]])
                dv(lambda e, xg=xg, sg=sg, x2g=x2g: e.tensor_add(
                    x2g, xg, sg), [b_x[og], b_s35[og]], [b_x2b[og]])
            x2_aps = [x2b_t[:, g * CL:(g + 1) * CL] for g in range(NG)]
            pA4, pB4, bA4, bB4 = stat_round(x2_aps, b_x2b, "p4", False,
                                            ["p4r", "p4rC"])
            for og in range(NG):
                x2g = x2b_t[:, og * CL:(og + 1) * CL]
                fg = fin_t[:, og * CL:(og + 1) * CL]
                tm = tmp32[og % 2]
                dv(lambda e, x2g=x2g, tm=tm, pA4=pA4: e.tensor_mul(
                    tm[:], x2g, pA4[:]), [b_x2b[og], bA4], [b_tmp[og % 2]])
                dv(lambda e, tm=tm, fg=fg, pB4=pB4: e.tensor_sub(
                    fg, tm[:], pB4[:]), [b_tmp[og % 2], bB4], [b_fin[og]])
            npair = None
            if nxt is not None:
                emit_xload(*nxt, nxtp)
                npair = front(*nxt, nxtp)
                xnmix(*nxt, nxtp, npair)
            for og in range(NFG):
                s1, s3 = (og * 2) % 4, (og * 2 + 1) % 4
                wload(s1, W1S[og], b_wdram["w1"])
                wload(s3, W3S[og], b_wdram["w3"])
                ps1, bps1 = PS_POOL[s1], b_ps[s1]
                ps3, bps3 = PS_POOL[s3], b_ps[s3]
                mm_group(ps1, bps1, wbuf[s1], b_wbuf[s1], fin_t, b_fin)
                mm_group(ps3, bps3, wbuf[s3], b_wbuf[s3], fin_t, b_fin)
                hs = hids[:, (og % 2) * CL:(og % 2) * CL + CL]
                sc(lambda e, hs=hs, ps1=ps1: e.activation(hs, ps1[:],
                                                          AF.Silu),
                   [bps1], [b_hids[og % 2]])
                hg_ap = hid_t[:, og * CL:(og + 1) * CL]
                dv(lambda e, hs=hs, hg_ap=hg_ap, ps3=ps3: e.tensor_mul(
                    hg_ap, hs, ps3[:]),
                   [b_hids[og % 2], bps3], [b_hid[og]])
            for og in range(NG):
                pso, bpso = PS_POOL[og % 4], b_ps[og % 4]
                for q in range(4):
                    sl = (og * 4 + q) % 4
                    wsrc = W2S[og, :, q * 16:(q + 1) * 16, :]
                    wload(sl, wsrc, b_wdram["w2s"])
                    for kk in range(NG):
                        kf = q * 16 + kk
                        pe(lambda e, kf=kf, kk=kk, sl=sl, pso=pso: e.matmul(
                            pso[:], wbuf[sl][:, kk * 128:(kk + 1) * 128],
                            hid_t[:, kf * CL:(kf + 1) * CL],
                            start=(kf == 0), stop=(kf == NFG - 1)),
                           [b_wbuf[sl], b_hid[kf]],
                           [bpso] if kf in (0, NFG - 1) else [])
                fog = fout_t[:, og * CL:(og + 1) * CL]
                sc(lambda e, fog=fog, pso=pso: e.copy(fog, pso[:]),
                   [bpso], [b_fout[og]])
            fo_aps = [fout_t[:, g * CL:(g + 1) * CL] for g in range(NG)]
            pA5, pB5, bA5, bB5 = stat_round(fo_aps, b_fout, "p5", False,
                                            ["p5r", "p5rC"])
            for half in range(2):
                for lo in range(8):
                    og = half * 8 + lo
                    fog = fout_t[:, og * CL:(og + 1) * CL]
                    sg = s35_t[:, og * CL:(og + 1) * CL]
                    ot = outst[:, lo * CL:(lo + 1) * CL]
                    xg = xt[:, og * CL:(og + 1) * CL]
                    tm = tmp32[og % 2]
                    dv(lambda e, fog=fog, tm=tm, pA5=pA5: e.tensor_mul(
                        tm[:], fog, pA5[:]),
                       [b_fout[og], bA5], [b_tmp[og % 2]])
                    dv(lambda e, tm=tm, pB5=pB5: e.tensor_sub(
                        tm[:], tm[:], pB5[:]),
                       [b_tmp[og % 2], bB5], [b_tmp[og % 2]])
                    dv(lambda e, tm=tm, sg=sg: e.tensor_add(tm[:], tm[:],
                                                            sg),
                       [b_tmp[og % 2], b_s35[og]], [b_tmp[og % 2]])
                    dv(lambda e, tm=tm, ot=ot, xg=xg: e.tensor_add(
                        ot, tm[:], xg),
                       [b_tmp[og % 2], b_x[og]], [b_out[lo]])
                tok = None
                for lo in range(8):
                    og = half * 8 + lo
                    ot = outst[:, lo * CL:(lo + 1) * CL]
                    tok = P.op("sc", lambda e, b=b, og=og, sr=sr, ot=ot:
                               e.dma_start(
                                   out=OUT[b, og, :,
                                           sr * CL:(sr + 1) * CL],
                                   in_=ot),
                               [b_out[lo]], [b_io], dma=True, sem="ost")
                for lo in range(8):
                    b_out[lo].r = [tok]
            if nxt is not None:
                projscan(*nxt)
            return npair

        if nb == 4:
            order = [(0, 0), (0, 1), (0, 2), (0, 3), (1, 0), (0, 4), (1, 1),
                     (0, 5), (1, 2), (1, 3), (2, 0), (1, 4), (2, 1), (1, 5),
                     (2, 2), (2, 3), (3, 0), (2, 4), (3, 1), (2, 5), (3, 2),
                     (3, 3), (3, 4), (3, 5)]
        else:
            order = [(bb, ss) for bb in range(nb) for ss in range(NCH)]
        assert sorted(order) == sorted(
            (bb, ss) for bb in range(nb) for ss in range(NCH))
        for bb in range(nb):
            seq = [ss for (b2, ss) in order if b2 == bb]
            assert seq == sorted(seq)

        xpar = {t: i % 2 for i, t in enumerate(order)}
        emit_xload(*order[0], xpar[order[0]])
        fpair = front(*order[0], xpar[order[0]])
        xnmix(*order[0], xpar[order[0]], fpair)
        projscan(*order[0])
        done_a = {order[0]}
        ci = 0
        while ci < len(order):
            bb, ss = order[ci]
            if (bb, ss) not in done_a:
                emit_xload(bb, ss, xpar[(bb, ss)])
                fpair = front(bb, ss, xpar[(bb, ss)])
                xnmix(bb, ss, xpar[(bb, ss)], fpair)
                projscan(bb, ss)
                done_a.add((bb, ss))
                continue
            if ss < NWARM:
                ci += 1
                if do_fold and (bb, ss) == (0, 1):
                    emit_fold()
                continue
            nxt = order[ci + 1] if ci + 1 < len(order) else None
            if nxt is not None and nxt in done_a:
                nxt = None
            np_ = body(bb, ss, xpar[(bb, ss)],
                       xpar[nxt] if nxt is not None else 0, nxt)
            if nxt is not None:
                fpair = np_
                done_a.add(nxt)
            ci += 1

        P.finalize()
        dma_keys = [k for k in P.final_counts
                    if k not in ("pe", "dv", "sc", "gp")]
        dma_fin = [(k, P.final_counts[k]) for k in dma_keys]

        @block.sync
        def _(sy):
            P.emit("sy", sy, sems)

        @block.gpsimd
        def _(gpe):
            P.emit("gp", gpe, sems)
            for k, v in dma_fin:
                gpe.wait_ge(sems[k], v)

        @block.tensor
        def _(te):
            P.emit("pe", te, sems)

        @block.vector
        def _(ve):
            P.emit("dv", ve, sems)

        @block.scalar
        def _(se):
            P.emit("sc", se, sems)

    return nc


def _hadamard(m):
    h = np.array([[1.0]], np.float32)
    while h.shape[0] < m:
        h = np.block([[h, h], [h, -h]])
    return np.ascontiguousarray(h, np.float32)


def _prep_consts(inp):
    def tile_w(w):
        o, i = w.shape
        return np.ascontiguousarray(
            w.reshape(o // 128, 128, i // 128, 128).transpose(0, 3, 2, 1))

    def colv(v):
        return np.ascontiguousarray(v.reshape(NG, 128).T)

    return {
        "wkT": tile_w(inp["L0_attn_Wk"]), "wvT": tile_w(inp["L0_attn_Wv"]),
        "wrT": tile_w(inp["L0_attn_Wr"]), "woT": tile_w(inp["L0_attn_Wo"]),
        "w1T": tile_w(inp["L0_ffn_W1"]), "w3T": tile_w(inp["L0_ffn_W3"]),
        "w2A": np.ascontiguousarray(
            inp["L0_ffn_W2"].reshape(H, 64, 128).transpose(1, 2, 0)),
        "lnw": colv(inp["L0_attn_ln_w"]), "lnb": colv(inp["L0_attn_ln_b"]),
        "mk": colv(inp["L0_attn_mix_k"]), "mv": colv(inp["L0_attn_mix_v"]),
        "mr": colv(inp["L0_attn_mix_r"]),
        "td": colv(inp["L0_attn_time_decay"]),
        "uu": colv(inp["L0_attn_current_token_weight"]),
        "h64": _hadamard(64),
        "h128s": np.ascontiguousarray(_hadamard(128) / np.sqrt(8192.0),
                                      np.float32),
        "onesc": np.full((128, 1), 1.0 / 2048, np.float32),
        "onesr": np.ones((1, 128), np.float32),
    }


def kernel(**inp):
    x = np.ascontiguousarray(inp["L0_attn_hidden_state"])
    nb = 4
    consts = _prep_consts(inp)
    nc = build()
    in_maps = []
    for c in range(NCORE):
        if c == 0:
            xs = np.concatenate(
                [np.zeros((nb, WARM, H), np.float32), x[:, 0:TREAL]], axis=1)
        else:
            xs = x[:, c * TREAL - WARM:c * TREAL + TREAL]
        xs = np.ascontiguousarray(
            xs.reshape(nb, WARM + TREAL, NG, 128).transpose(0, 2, 3, 1))
        m = dict(consts)
        m["xs"] = xs
        m["kmask"] = np.full((128, 1), -1e9 if c == 0 else 0.0, np.float32)
        m["gmask"] = np.full((128, 1), 0.0 if c == 0 else 1.0, np.float32)
        in_maps.append(m)
    res = run_bass_kernel_spmd(nc, in_maps, core_ids=list(range(NCORE)))
    outs = []
    for c in range(NCORE):
        o = np.asarray(res.results[c]["out"])
        outs.append(np.ascontiguousarray(
            o.transpose(0, 3, 1, 2).reshape(nb, TREAL, H)))
    return np.concatenate(outs, axis=1)


# version 23
# speedup vs baseline: 1.4044x; 1.4044x over previous
import numpy as np
from contextlib import ExitStack

import concourse.bass as bass
import concourse.mybir as mybir
from concourse.bass_utils import run_bass_kernel_spmd

F32 = mybir.dt.float32
BF16 = mybir.dt.bfloat16
AF = mybir.ActivationFunctionType
ALU = mybir.AluOpType
AX = mybir.AxisListType

H = 2048
NG = 16
NFG = 64
CL = 256
NWARM = 2
NREAL = 4
WARM = NWARM * CL
TREAL = NREAL * CL
NCH = NWARM + NREAL
NCORE = 8


class Buf:
    __slots__ = ("w", "r", "name")

    def __init__(self, name=""):
        self.w = None
        self.r = []
        self.name = name


class Prog:
    def __init__(self):
        self.items = []
        self.cnt = {}
        self.hw = {}
        self.waited = set()

    def op(self, eng, fn, reads=(), writes=(), dma=False, sem=None):
        semkey = sem if sem else (eng + "d" if dma else eng)
        amt = 16 if dma else 1
        need = {}

        def add(tok):
            if tok is not None and need.get(tok[0], -1) < tok[1]:
                need[tok[0]] = tok[1]

        for b in reads:
            add(b.w)
        for b in writes:
            add(b.w)
            for t in b.r:
                add(t)
        v = self.cnt.get(semkey, 0) + amt
        self.cnt[semkey] = v
        tok = (semkey, v)
        waits = []
        for sk, nv in sorted(need.items()):
            if eng == "pe" and sk == "pe":
                continue
            if self.hw.get((eng, sk), 0) >= nv:
                continue
            waits.append((sk, nv))
            self.hw[(eng, sk)] = nv
            self.waited.add((sk, nv))
        self.items.append([eng, fn, waits, semkey, amt, v])
        for b in reads:
            b.r.append(tok)
        for b in writes:
            b.w = tok
            b.r = []
        return tok

    def finalize(self):
        keep, newv, cnt = {}, {}, {}
        for it in self.items:
            _, _, _, semkey, amt, v = it
            tk = (semkey, v)
            if amt == 16 or tk in self.waited:
                cnt[semkey] = cnt.get(semkey, 0) + amt
                keep[tk] = True
                newv[tk] = cnt[semkey]
            else:
                keep[tk] = False
        for it in self.items:
            it[2] = [(sk, newv[(sk, v)]) for sk, v in it[2]]
            it[5] = newv[(it[3], it[5])] if keep[(it[3], it[5])] else None
        self.final_counts = cnt

    def emit(self, engname, engobj, sems):
        for eng, fn, waits, semkey, amt, newval in self.items:
            if eng != engname:
                continue
            for sk, v in waits:
                engobj.wait_ge(sems[sk], v)
            inst = fn(engobj)
            if newval is not None:
                inst.then_inc(sems[semkey], amt)


STAT = {"mu": 0, "v": 1, "t": 2, "t2": 3,
        "fA": 4, "fAC": 5, "fr": 6, "frC": 7,
        "p3r": 8, "p3rC": 9, "p4r": 8, "p4rC": 9,
        "p5r": 8, "p5rC": 9}


def build(nb=4, do_fold=True):
    TL = NCH * CL
    nc = bass.Bass("TRN2", target_bir_lowering=False)

    def din(name, shape, dtype=F32):
        return nc.dram_tensor(name, shape, dtype, kind="ExternalInput").ap()

    XS = din("xs", [nb, NG, 128, TL])
    WKT = din("wkT", [NG, 128, NG, 128])
    WVT = din("wvT", [NG, 128, NG, 128])
    WRT = din("wrT", [NG, 128, NG, 128])
    WOT = din("woT", [NG, 128, NG, 128])
    W1T = din("w1T", [NFG, 128, NG, 128])
    W3T = din("w3T", [NFG, 128, NG, 128])
    W2A = din("w2A", [64, 128, H])
    LNW = din("lnw", [128, NG])
    LNB = din("lnb", [128, NG])
    MKc = din("mk", [128, NG])
    MVc = din("mv", [128, NG])
    MRc = din("mr", [128, NG])
    TDc = din("td", [128, NG])
    UUc = din("uu", [128, NG])
    KM = din("kmask", [128, 1])
    GM = din("gmask", [128, 1])
    H64 = din("h64", [64, 64])
    H128S = din("h128s", [128, 128])
    ONESC = din("onesc", [128, 1])
    ONESR = din("onesr", [1, 128])

    OUT = nc.dram_tensor("out", [nb, NG, 128, TREAL], F32,
                         kind="ExternalOutput").ap()
    WKS = nc.dram_tensor("wkS", [NG, 128, NG, 128], BF16).ap()
    WVS = nc.dram_tensor("wvS", [NG, 128, NG, 128], BF16).ap()
    WRS = nc.dram_tensor("wrS", [NG, 128, NG, 128], BF16).ap()
    WOS = nc.dram_tensor("woS", [NG, 128, NG, 128], BF16).ap()
    W1S = nc.dram_tensor("w1S", [NFG, 128, NG, 128], BF16).ap()
    W3S = nc.dram_tensor("w3S", [NFG, 128, NG, 128], BF16).ap()
    W2AB = nc.dram_tensor("w2Ab", [64, 128, H], BF16).ap()
    SCRA = nc.dram_tensor("scrA", [64, 128, H], BF16).ap()
    W2S = nc.dram_tensor("w2S", [NG, 128, NFG, 128], BF16).ap()

    P = Prog()

    with ExitStack() as es:
        sb_base = nc._sbuf_addr_for_side("left")
        sb_top = sb_base + nc.sbuf_bytes_remaining
        cur = [sb_base]

        def A(name, shape, dtype):
            per = int(np.prod(shape[1:])) * mybir.dt.size(dtype)
            off = (cur[0] + 63) & ~63
            cur[0] = off + per
            assert off + per <= sb_top, f"sbuf overflow {name} {off + per}"
            return nc.alloc_sbuf_tensor_at(name, shape, dtype,
                                           offset=off).ap()

        x_t = A("x_t", [128, NG * CL], F32)
        sqb = A("sqb", [128, 2 * CL], BF16)
        strow = [A(f"strow{i}", [1, CL], F32) for i in range(10)]
        hirow = A("hirow", [1, CL], BF16)
        lorow = A("lorow", [1, CL], BF16)
        wbuf = [A(f"wbuf{i}", [128, NG * 128], BF16) for i in range(4)]
        outst = A("outst", [128, 8 * CL], F32)
        x_t2 = A("x_t2", [128, NG * CL], F32)
        tmp32 = [A(f"tmp32_{i}", [128, CL], F32) for i in range(2)]
        dmix = A("dmix", [128, CL], BF16)
        lnw_t = A("lnw_t", [128, NG], F32)
        lnb_t = A("lnb_t", [128, NG], F32)
        mk_t = A("mk_t", [128, NG], F32)
        mv_t = A("mv_t", [128, NG], F32)
        mr_t = A("mr_t", [128, NG], F32)
        td_t = A("td_t", [128, NG], F32)
        uu_t = A("uu_t", [128, NG], F32)
        lam_t = A("lam_t", [128, NG], F32)
        etmp_t = A("etmp_t", [128, NG], F32)
        km_t = A("km_t", [128, 1], F32)
        gm_t = A("gm_t", [128, 1], F32)
        onesc_b = A("onesc_b", [128, 1], BF16)
        onesr_b = A("onesr_b", [1, 128], BF16)
        h64_b = A("h64_b", [64, 64], BF16)
        h128s_b = A("h128s_b", [128, 128], BF16)
        carr_m = A("carr_m", [128, 4 * NG], F32)
        carr_D = A("carr_D", [128, 4 * NG], F32)
        carr_E = A("carr_E", [128, 4 * NG], F32)
        scol = A("scol", [128, 8], F32)
        epsc = A("epsc", [1, 8], F32)
        guard_t = A("guard_t", [128, 4 * NG], BF16)
        xn_t = A("xn_t", [128, NG * (CL + 1)], BF16)
        xk_t = A("xk_t", [128, NG * CL], BF16)
        xv_t = A("xv_t", [128, NG * CL], BF16)
        xr_t = A("xr_t", [128, NG * CL], BF16)
        kq_t = A("kq_t", [128, 2 * CL], F32)
        vq_t = A("vq_t", [128, 2 * CL], F32)
        p_t = A("p_t", [128, CL], F32)
        pv_t = A("pv_t", [128, CL], F32)
        bon_t = A("bon_t", [128, CL], F32)
        D_t = A("D_t", [128, CL + 1], F32)
        E_t = A("E_t", [128, CL + 1], F32)
        r_t = A("r_t", [128, NG * CL], BF16)
        ao_t = A("ao_t", [128, NG * CL], BF16)
        xbf_t = ao_t
        s35_t = A("s35_t", [128, NG * CL], BF16)
        x2b_t = A("x2b_t", [128, NG * CL], BF16)
        fin_t = A("fin_t", [128, NG * CL], BF16)
        fout_t = fin_t
        hid_t = A("hid_t", [128, NFG * CL], BF16)
        hids = A("hids", [128, 2 * CL], BF16)
        fA = A("fA", [128, 2048], BF16)
        fB = A("fB", [128, NG * 128], BF16)
        fE = A("fE", [128, 2 * 2048], BF16)

        psf = [es.enter_context(nc.psum_tensor(f"ps{i}", [128, 512], F32))
               for i in range(8)]
        ps = [t[:, 0:CL] for t in psf]
        PS_POOL = ps[0:4]
        rot = [0]

        semkeys = (["pe", "dv", "sc", "gp", "gpd", "x0", "x1", "ost",
                    "fa", "fb", "fs0", "fs1", "ws"] +
                   [f"w{i}" for i in range(4)] +
                   [f"c{i}" for i in range(8)])
        sems = {k: es.enter_context(nc.semaphore(f"s_{k}")) for k in semkeys}
        block = es.enter_context(nc.Block())

        def bufs(pref, n):
            return [Buf(f"{pref}{i}") for i in range(n)]

        b_x2 = [bufs("xA", NG), bufs("xB", NG)]

        b_xn = bufs("xn", NG)
        b_xk = bufs("xk", NG)
        b_xv = bufs("xv", NG)
        b_xr = bufs("xr", NG)
        b_kq = bufs("kq", 2)
        b_vq = bufs("vq", 2)
        b_p, b_pv, b_bon = Buf("p"), Buf("pv"), Buf("bon")
        b_D, b_E = Buf("D"), Buf("E")
        b_r = bufs("r", NG)
        b_ao = bufs("ao", NG)
        b_xbf = b_ao
        b_s35 = bufs("s35", NG)
        b_x2b = bufs("x2b", NG)
        b_fin = bufs("fin", NG)
        b_fout = b_fin
        b_hid = bufs("hid", NFG)
        b_hids = bufs("hids", 2)
        b_sq = bufs("sq", 2)
        b_wbuf = bufs("wbuf", 4)
        b_out = bufs("outst", 8)
        b_tmp = bufs("tmp32_", 2)
        b_dmix = Buf("dmix")
        b_ps = [Buf(f"ps{i}") for i in range(8)]
        b_row = [Buf(f"row{i}") for i in range(10)]
        b_hi, b_lo = Buf("hirow"), Buf("lorow")
        b_carr = bufs("carr", 4 * NG)
        b_scol = Buf("scol")
        b_consts = Buf("consts")
        b_guard = bufs("guard", 4)
        b_fA = Buf("fA")
        b_fB = Buf("fB")
        b_fE = bufs("fE", 4)
        b_scrA = Buf("scrA")
        b_scrA2 = Buf("scrA2")
        b_wdram = {n: Buf("wd_" + n) for n in
                   ["wk", "wv", "wr", "wo", "w1", "w3", "w2ab", "w2s"]}
        b_io = Buf("io")

        def pair(peek=False):
            i = rot[0] % 2
            if not peek:
                rot[0] += 1
            return (ps[4 + i * 2], ps[5 + i * 2],
                    b_ps[4 + i * 2], b_ps[5 + i * 2])

        def row(i):
            return strow[i][:]

        def dv(fn, reads, writes):
            return P.op("dv", fn, reads, writes)

        def sc(fn, reads, writes):
            return P.op("sc", fn, reads, writes)

        def pe(fn, reads, writes):
            return P.op("pe", fn, reads, writes)

        def gp(fn, reads, writes):
            return P.op("gp", fn, reads, writes)

        def gpd(fn, reads, writes, sem=None):
            return P.op("gp", fn, reads, writes, dma=True, sem=sem)

        def syd(fn, reads, writes, sem):
            return P.op("sy", fn, reads, writes, dma=True, sem=sem)

        for src, dst in ((LNW, lnw_t), (LNB, lnb_t), (MKc, mk_t),
                         (MVc, mv_t), (MRc, mr_t), (TDc, td_t), (UUc, uu_t),
                         (KM, km_t), (GM, gm_t), (H64, h64_b),
                         (H128S, h128s_b), (ONESC, onesc_b),
                         (ONESR, onesr_b)):
            gpd(lambda e, s=src, d=dst: e.dma_start(out=d[:], in_=s),
                [], [b_consts])
        for i, v in enumerate((1e-10, 1e-5, 1.00001, -0.5)):
            dv(lambda e, i=i, v=v: e.memset(epsc[0:1, i:i + 1], v),
               [], [b_consts])
        sc(lambda e: e.activation(etmp_t[:], td_t[:], AF.Exp),
           [b_consts], [b_consts])
        sc(lambda e: e.activation(lam_t[:], etmp_t[:], AF.Exp, scale=-1.0),
           [b_consts], [b_consts])

        castn = [0]

        def cast_w(src, dst, bname):
            gpd(lambda e, s=src, d=dst: e.dma_start(out=d, in_=s),
                [], [b_wdram[bname]], sem=f"c{castn[0]}")
            castn[0] += 1

        for csrc, cdst, cnm in [(WKT, WKS, "wk"), (WVT, WVS, "wv"),
                                (W2A, W2AB, "w2ab"), (WRT, WRS, "wr"),
                                (WOT, WOS, "wo"), (W1T, W1S, "w1"),
                                (W3T, W3S, "w3")]:
            cast_w(csrc, cdst, cnm)

        def bcast(ri, psd, psb):
            rap = row(ri)
            rb = b_row[ri]
            dv(lambda e: e.tensor_copy(hirow[:], rap), [rb], [b_hi])
            dv(lambda e: e.tensor_sub(lorow[:], rap, hirow[:]),
               [rb, b_hi], [b_lo])
            pe(lambda e: e.matmul(psd[:], onesr_b[:], hirow[:],
                                  start=True, stop=False),
               [b_consts, b_hi], [psb])
            pe(lambda e: e.matmul(psd[:], onesr_b[:], lorow[:],
                                  start=False, stop=True),
               [b_consts, b_lo], [psb])

        def stats(aps, bbs, pA, pB, bA, bB):
            for g in range(NG):
                sq = sqb[:, (g % 2) * CL:(g % 2) * CL + CL]
                sc(lambda e, g=g, sq=sq: e.activation(sq, aps[g], AF.Square),
                   [bbs[g]], [b_sq[g % 2]])
                pe(lambda e, g=g, pA=pA: e.matmul(
                    pA[0:1, :], onesc_b[:], aps[g],
                    start=(g == 0), stop=(g == NG - 1)),
                   [b_consts, bbs[g]],
                   [bA] if g in (0, NG - 1) else [])
                pe(lambda e, g=g, sq=sq, pB=pB: e.matmul(
                    pB[0:1, :], onesc_b[:], sq,
                    start=(g == 0), stop=(g == NG - 1)),
                   [b_consts, b_sq[g % 2]],
                   [bB] if g in (0, NG - 1) else [])

        def stat_math(pref, inner, pA, pB, bA, bB):
            iu, iv = STAT["mu"], STAT["v"]
            it, it2 = STAT["t"], STAT["t2"]
            dv(lambda e, pA=pA: e.tensor_copy(row(iu), pA[0:1, :]),
               [bA], [b_row[iu]])
            dv(lambda e: e.tensor_mul(row(it), row(iu), row(iu)),
               [b_row[iu]], [b_row[it]])
            dv(lambda e, pB=pB: e.tensor_sub(row(iv), pB[0:1, :], row(it)),
               [bB, b_row[it]], [b_row[iv]])
            specs = ([(pref + "A", True)] if inner else []) + \
                [(pref + "r", False)]
            for nm, isA in specs:
                io, ic = STAT[nm], STAT[nm + "C"]
                bia = epsc[0:1, 0:1] if isA else epsc[0:1, 1:2]
                sca = epsc[0:1, 2:3] if isA else 1.0
                sc(lambda e, bia=bia, sca=sca: e.activation(
                    row(it2), row(iv), AF.Ln, bias=bia, scale=sca),
                   [b_row[iv], b_consts], [b_row[it2]])
                sc(lambda e, io=io: e.activation(
                    row(io), row(it2), AF.Exp, scale=epsc[0:1, 3:4]),
                   [b_row[it2], b_consts], [b_row[io]])
                dv(lambda e, io=io, ic=ic: e.tensor_mul(row(ic), row(iu),
                                                        row(io)),
                   [b_row[iu], b_row[io]], [b_row[ic]])

        def stat_round(aps, bbs, pref, inner, bnames):
            pA, pB, bA, bB = pair()
            stats(aps, bbs, pA, pB, bA, bB)
            stat_math(pref, inner, pA, pB, bA, bB)
            bcast(STAT[bnames[0]], pA, bA)
            bcast(STAT[bnames[1]], pB, bB)
            return pA, pB, bA, bB

        def mm_group(psd, psb, w_ap, wslot, rhs_t, rhs_bufs, n=NG):
            for kk in range(n):
                pe(lambda e, kk=kk: e.matmul(
                    psd[:], w_ap[:, kk * 128:(kk + 1) * 128],
                    rhs_t[:, kk * CL:(kk + 1) * CL],
                    start=(kk == 0), stop=(kk == n - 1)),
                   [wslot, rhs_bufs[kk]],
                   [psb] if kk in (0, n - 1) else [])

        def wload(sl, src, src_buf):
            eng = "sy" if sl < 2 else "gp"
            P.op(eng, lambda e: e.dma_start(
                out=wbuf[sl][:].rearrange("p (k m) -> p k m", m=128),
                in_=src),
                [src_buf], [b_wbuf[sl]], dma=True, sem=f"w{sl}")

        def emit_fold():
            for pp in range(128):
                sl = pp % 2
                fa = fA[0:64, :]
                fe = fE[0:64, sl * 2048:(sl + 1) * 2048]
                syd(lambda e, pp=pp, fa=fa: e.dma_start(
                    out=fa, in_=W2AB[:, pp, :]),
                    [b_wdram["w2ab"]], [b_fA], sem="fa")
                for d8 in range(8):
                    pst, pb = ps[d8 % 8], b_ps[d8 % 8]
                    feb = b_fE[sl * 2 + d8 % 2]
                    pe(lambda e, fa=fa, d8=d8, pst=pst: e.matmul(
                        pst[0:64, :], h64_b[:],
                        fa[:, d8 * 256:d8 * 256 + 256],
                        start=True, stop=True),
                       [b_consts, b_fA], [pb])
                    if d8 % 2:
                        sc(lambda e, fe=fe, d8=d8, pst=pst: e.copy(
                            fe[:, d8 * 256 - 2048 * 0:(d8 + 1) * 256]
                            if False else fe[:, d8 * 256:(d8 + 1) * 256],
                            pst[0:64, :]), [pb], [feb])
                    else:
                        dv(lambda e, fe=fe, d8=d8, pst=pst: e.tensor_copy(
                            fe[:, d8 * 256:(d8 + 1) * 256],
                            pst[0:64, :]), [pb], [feb])
                syd(lambda e, pp=pp, fe=fe, sl=sl: e.dma_start(
                    out=SCRA[:, pp, :], in_=fe),
                    [b_fE[sl * 2], b_fE[sl * 2 + 1]],
                    [b_scrA if sl == 0 else b_scrA2],
                    sem=f"fs{sl}")
            for g in range(64):
                syd(lambda e, g=g: e.dma_start(out=fB[:], in_=SCRA[g]),
                    [b_scrA, b_scrA2], [b_fB], sem="fb")
                for d8 in range(8):
                    sl = d8 % 4
                    pst, pb = ps[sl], b_ps[sl]
                    pe(lambda e, d8=d8, pst=pst: e.matmul(
                        pst[:], h128s_b[:],
                        fB[:, d8 * 256:d8 * 256 + 256],
                        start=True, stop=True),
                       [b_consts, b_fB], [pb])
                    fe = fE[:, d8 * 256:(d8 + 1) * 256]
                    sc(lambda e, fe=fe, pst=pst: e.copy(fe, pst[:]),
                       [pb], [b_fE[sl]])

                    def store(e, g=g, d8=d8, fe=fe):
                        dstv = W2S.rearrange(
                            "ogd p kf m -> p kf ogd m")[:, g,
                                                        d8 * 2:(d8 + 1) * 2]
                        srcv = fe.rearrange("p (ogd m) -> p ogd m", m=128)
                        return e.dma_start(out=dstv, in_=srcv)
                    syd(store, [b_fE[sl]], [b_wdram["w2s"]], sem="ws")

        def emit_xload(b, s, xp):
            t0 = s * CL
            xt = x_t if xp == 0 else x_t2
            tok = None
            for g in range(NG):
                tok = P.op("sc", lambda e, b=b, g=g, t0=t0, xt=xt:
                           e.dma_start(out=xt[:, g * CL:(g + 1) * CL],
                                       in_=XS[b, g, :, t0:t0 + CL]),
                           [], [b_x2[xp][g]], dma=True, sem=f"x{xp}")
            for g in range(NG):
                b_x2[xp][g].w = tok

        def front(b, s, xp):
            xt = x_t if xp == 0 else x_t2
            for g in range(NG):
                gp(lambda e, g=g, xt=xt: e.tensor_copy(
                    xbf_t[:, g * CL:(g + 1) * CL],
                    xt[:, g * CL:(g + 1) * CL]), [b_x2[xp][g]], [b_xbf[g]])
            xbf_aps = [xbf_t[:, g * CL:(g + 1) * CL] for g in range(NG)]
            return stat_round(xbf_aps, b_xbf, "f", True, ["fA", "fAC"])

        def xnmix(b, s, xp, fpair):
            warm = s < NWARM
            xt = x_t if xp == 0 else x_t2
            b_x = b_x2[xp]
            pA, pB, bA, bB = fpair
            cb = b * NG
            bslc = slice(cb, cb + NG)
            if s == 0:
                dv(lambda e, bslc=bslc: e.memset(carr_m[:, bslc], -1e30),
                   [], b_carr[cb:cb + NG])
                dv(lambda e, bslc=bslc: e.memset(carr_D[:, bslc], 0.0),
                   [], b_carr[cb:cb + NG])
                dv(lambda e, bslc=bslc: e.memset(carr_E[:, bslc], 0.0),
                   [], b_carr[cb:cb + NG])
                dv(lambda e, bslc=bslc: e.memset(guard_t[:, bslc], 0.0),
                   [], [b_guard[b]])
            if s == NWARM:
                dv(lambda e, bslc=bslc: e.tensor_scalar_mul(
                    xn_t[:].rearrange("p (g c) -> p g c",
                                      c=CL + 1)[:, :, 0],
                    guard_t[:, bslc], gm_t[:]),
                   [b_guard[b], b_consts], b_xn)
            else:
                dv(lambda e, bslc=bslc: e.tensor_copy(
                    xn_t[:].rearrange("p (g c) -> p g c",
                                      c=CL + 1)[:, :, 0],
                    guard_t[:, bslc]), [b_guard[b]], b_xn)
            for g in range(NG):
                xg0 = xt[:, g * CL:(g + 1) * CL]
                xng = xn_t[:, g * (CL + 1) + 1:(g + 1) * (CL + 1)]
                sxg = xn_t[:, g * (CL + 1):g * (CL + 1) + CL]
                tm = tmp32[g % 2]
                dv(lambda e, xg0=xg0, tm=tm, pA=pA: e.tensor_mul(
                    tm[:], xg0, pA[:]), [b_x[g], bA], [b_tmp[g % 2]])
                dv(lambda e, tm=tm, pB=pB: e.tensor_sub(tm[:], tm[:],
                                                        pB[:]),
                   [b_tmp[g % 2], bB], [b_tmp[g % 2]])
                dv(lambda e, g=g, xng=xng, tm=tm: e.tensor_scalar(
                    xng, tm[:], lnw_t[:, g:g + 1], lnb_t[:, g:g + 1],
                    ALU.mult, ALU.add),
                   [b_tmp[g % 2], b_consts], [b_xn[g]])
                dv(lambda e, xng=xng, sxg=sxg: e.tensor_sub(dmix[:], xng,
                                                            sxg),
                   [b_xn[g]], [b_dmix])
                mixes = [(mk_t, xk_t, b_xk), (mv_t, xv_t, b_xv)]
                if not warm:
                    mixes.append((mr_t, xr_t, b_xr))
                for mt, dstt, bb2 in mixes:
                    dv(lambda e, g=g, mt=mt, dstt=dstt, sxg=sxg:
                       e.scalar_tensor_tensor(
                           dstt[:, g * CL:(g + 1) * CL], dmix[:],
                           mt[:, g:g + 1], sxg, ALU.mult, ALU.add),
                       [b_dmix, b_xn[g], b_consts], [bb2[g]])
            dv(lambda e, bslc=bslc: e.tensor_copy(
                guard_t[:, bslc],
                xn_t[:].rearrange("p (g c) -> p g c", c=CL + 1)[:, :, CL]),
               b_xn, [b_guard[b]])

        def projscan(b, s):
            warm = s < NWARM
            cb = b * NG

            def proj_emit(hg):
                par = hg % 2
                sk, sv = (hg * 2) % 4, (hg * 2 + 1) % 4
                wload(sk, WKS[hg], b_wdram["wk"])
                wload(sv, WVS[hg], b_wdram["wv"])
                psk, bpsk = PS_POOL[2 * par], b_ps[2 * par]
                psv, bpsv = PS_POOL[2 * par + 1], b_ps[2 * par + 1]
                mm_group(psk, bpsk, wbuf[sk], b_wbuf[sk], xk_t, b_xk)
                mm_group(psv, bpsv, wbuf[sv], b_wbuf[sv], xv_t, b_xv)
                kq = kq_t[:, par * CL:par * CL + CL]
                vq = vq_t[:, par * CL:par * CL + CL]
                sc(lambda e, kq=kq, psk=psk: e.copy(kq, psk[:]),
                   [bpsk], [b_kq[par]])
                sc(lambda e, vq=vq, psv=psv: e.copy(vq, psv[:]),
                   [bpsv], [b_vq[par]])
                if not warm:
                    wload(sk, WRS[hg], b_wdram["wr"])
                    rA, rB, rbA, rbB = pair(peek=True)
                    psr, bpsr = (rA, rbA) if par == 0 else (rB, rbB)
                    mm_group(psr, bpsr, wbuf[sk], b_wbuf[sk], xr_t, b_xr)
                    rg = r_t[:, hg * CL:(hg + 1) * CL]
                    sc(lambda e, rg=rg, psr=psr: e.activation(
                        rg, psr[:], AF.Sigmoid), [bpsr], [b_r[hg]])

            def scan_emit(hg):
                par = hg % 2
                kq = kq_t[:, par * CL:par * CL + CL]
                vq = vq_t[:, par * CL:par * CL + CL]
                lm, mc, negm = scol[:, 0:1], scol[:, 1:2], scol[:, 2:3]
                ub, cs = scol[:, 3:4], scol[:, 4:5]
                mp = carr_m[:, cb + hg:cb + hg + 1]
                Dc = carr_D[:, cb + hg:cb + hg + 1]
                Ec = carr_E[:, cb + hg:cb + hg + 1]
                bca = b_carr[cb + hg]
                dv(lambda e, kq=kq: e.reduce_max(lm, kq, axis=AX.X),
                   [b_kq[par]], [b_scol])
                if warm:
                    dv(lambda e: e.tensor_add(lm, lm, km_t[:]),
                       [b_scol, b_consts], [b_scol])
                dv(lambda e, mp=mp: e.tensor_max(mc, lm, mp),
                   [b_scol, bca], [b_scol])
                dv(lambda e: e.tensor_scalar_mul(negm, mc, -1.0),
                   [b_scol], [b_scol])
                pb = scol[:, 5:6]
                if warm:
                    dv(lambda e: e.tensor_add(pb, negm, km_t[:]),
                       [b_scol, b_consts], [b_scol])
                else:
                    pb = negm
                sc(lambda e, kq=kq, pb=pb: e.activation(p_t[:], kq, AF.Exp,
                                                        bias=pb),
                   [b_kq[par], b_scol], [b_p])
                dv(lambda e, vq=vq: e.tensor_mul(pv_t[:], p_t[:], vq),
                   [b_p, b_vq[par]], [b_pv])
                sc(lambda e, mp=mp: e.activation(cs, mp, AF.Exp, bias=negm),
                   [bca, b_scol], [b_scol])
                dv(lambda e, Dc=Dc: e.tensor_mul(D_t[:, 0:1], Dc, cs),
                   [bca, b_scol], [b_D])
                dv(lambda e, Ec=Ec: e.tensor_mul(E_t[:, 0:1], Ec, cs),
                   [bca, b_scol], [b_E])
                lamb = lam_t[:, hg:hg + 1].broadcast_to([128, CL])
                dv(lambda e, lamb=lamb: e.tensor_tensor_scan(
                    D_t[:, 1:CL + 1], lamb, pv_t[:], D_t[:, 0:1],
                    ALU.mult, ALU.add),
                   [b_consts, b_pv, b_D], [b_D])
                dv(lambda e, lamb=lamb: e.tensor_tensor_scan(
                    E_t[:, 1:CL + 1], lamb, p_t[:], E_t[:, 0:1],
                    ALU.mult, ALU.add),
                   [b_consts, b_p, b_E], [b_E])
                dv(lambda e, Dc=Dc: e.tensor_copy(Dc, D_t[:, CL:CL + 1]),
                   [b_D], [bca])
                dv(lambda e, Ec=Ec: e.tensor_copy(Ec, E_t[:, CL:CL + 1]),
                   [b_E], [bca])
                dv(lambda e, mp=mp: e.tensor_copy(mp, mc),
                   [b_scol], [bca])
                if not warm:
                    dv(lambda e, hg=hg: e.tensor_add(
                        ub, uu_t[:, hg:hg + 1], negm),
                       [b_consts, b_scol], [b_scol])
                    sc(lambda e, kq=kq: e.activation(bon_t[:], kq, AF.Exp,
                                                     bias=ub),
                       [b_kq[par], b_scol], [b_bon])
                    dv(lambda e, vq=vq: e.tensor_mul(pv_t[:], bon_t[:], vq),
                       [b_bon, b_vq[par], b_pv], [b_pv])
                    dv(lambda e: e.tensor_add(pv_t[:], pv_t[:],
                                              D_t[:, 0:CL]),
                       [b_pv, b_D], [b_pv])
                    dv(lambda e: e.tensor_add(p_t[:], bon_t[:],
                                              E_t[:, 0:CL]),
                       [b_bon, b_E, b_p], [b_p])
                    sc(lambda e: e.activation(bon_t[:], p_t[:], AF.Ln),
                       [b_p], [b_bon])
                    sc(lambda e: e.activation(bon_t[:], bon_t[:], AF.Exp,
                                              scale=-1.0),
                       [b_bon], [b_bon])
                    dv(lambda e: e.tensor_mul(p_t[:], pv_t[:], bon_t[:]),
                       [b_pv, b_bon, b_p], [b_p])
                    rg = r_t[:, hg * CL:(hg + 1) * CL]
                    dv(lambda e, rg=rg: e.tensor_mul(rg, rg, p_t[:]),
                       [b_r[hg], b_p], [b_r[hg]])

            for hg in range(NG):
                proj_emit(hg)
                if hg > 0:
                    scan_emit(hg - 1)
            scan_emit(NG - 1)

        def body(b, s, xp, nxtp, nxt):
            xt = x_t if xp == 0 else x_t2
            b_x = b_x2[xp]
            sr = s - NWARM
            pA1, pB1, bA1, bB1 = pair()
            bcast(STAT["fr"], pA1, bA1)
            bcast(STAT["frC"], pB1, bB1)
            pA3, pB3, bA3, bB3 = pair()
            for og in range(NG):
                sl = og % 4
                wload(sl, WOS[og], b_wdram["wo"])
                pso, bpso = PS_POOL[sl], b_ps[sl]
                mm_group(pso, bpso, wbuf[sl], b_wbuf[sl], r_t, b_r)
                xg = xt[:, og * CL:(og + 1) * CL]
                aog = ao_t[:, og * CL:(og + 1) * CL]
                tm = tmp32[og % 2]
                dv(lambda e, xg=xg, tm=tm, pA1=pA1: e.tensor_mul(
                    tm[:], xg, pA1[:]), [b_x[og], bA1], [b_tmp[og % 2]])
                dv(lambda e, tm=tm, pso=pso: e.tensor_add(tm[:], tm[:],
                                                          pso[:]),
                   [b_tmp[og % 2], bpso], [b_tmp[og % 2]])
                dv(lambda e, tm=tm, aog=aog, pB1=pB1: e.tensor_sub(
                    aog, tm[:], pB1[:]), [b_tmp[og % 2], bB1], [b_ao[og]])
                sq = sqb[:, (og % 2) * CL:(og % 2) * CL + CL]
                sc(lambda e, sq=sq, aog=aog: e.activation(sq, aog,
                                                          AF.Square),
                   [b_ao[og]], [b_sq[og % 2]])
                pe(lambda e, og=og, aog=aog, pA3=pA3: e.matmul(
                    pA3[0:1, :], onesc_b[:], aog,
                    start=(og == 0), stop=(og == NG - 1)),
                   [b_consts, b_ao[og]],
                   [bA3] if og in (0, NG - 1) else [])
                pe(lambda e, og=og, sq=sq, pB3=pB3: e.matmul(
                    pB3[0:1, :], onesc_b[:], sq,
                    start=(og == 0), stop=(og == NG - 1)),
                   [b_consts, b_sq[og % 2]],
                   [bB3] if og in (0, NG - 1) else [])
            stat_math("p3", False, pA3, pB3, bA3, bB3)
            bcast(STAT["p3r"], pA3, bA3)
            bcast(STAT["p3rC"], pB3, bB3)
            for og in range(NG):
                aog = ao_t[:, og * CL:(og + 1) * CL]
                sg = s35_t[:, og * CL:(og + 1) * CL]
                xg = xt[:, og * CL:(og + 1) * CL]
                x2g = x2b_t[:, og * CL:(og + 1) * CL]
                tm = tmp32[og % 2]
                dv(lambda e, aog=aog, tm=tm, pA3=pA3: e.tensor_mul(
                    tm[:], aog, pA3[:]), [b_ao[og], bA3], [b_tmp[og % 2]])
                dv(lambda e, tm=tm, sg=sg, pB3=pB3: e.tensor_sub(
                    sg, tm[:], pB3[:]), [b_tmp[og % 2], bB3], [b_s35[og]])
                dv(lambda e, xg=xg, sg=sg, x2g=x2g: e.tensor_add(
                    x2g, xg, sg), [b_x[og], b_s35[og]], [b_x2b[og]])
            x2_aps = [x2b_t[:, g * CL:(g + 1) * CL] for g in range(NG)]
            pA4, pB4, bA4, bB4 = stat_round(x2_aps, b_x2b, "p4", False,
                                            ["p4r", "p4rC"])
            for og in range(NG):
                x2g = x2b_t[:, og * CL:(og + 1) * CL]
                fg = fin_t[:, og * CL:(og + 1) * CL]
                tm = tmp32[og % 2]
                dv(lambda e, x2g=x2g, tm=tm, pA4=pA4: e.tensor_mul(
                    tm[:], x2g, pA4[:]), [b_x2b[og], bA4], [b_tmp[og % 2]])
                dv(lambda e, tm=tm, fg=fg, pB4=pB4: e.tensor_sub(
                    fg, tm[:], pB4[:]), [b_tmp[og % 2], bB4], [b_fin[og]])
            npair = None
            if nxt is not None:
                emit_xload(*nxt, nxtp)
                npair = front(*nxt, nxtp)
                xnmix(*nxt, nxtp, npair)
            for og in range(NFG):
                s1, s3 = (og * 2) % 4, (og * 2 + 1) % 4
                wload(s1, W1S[og], b_wdram["w1"])
                wload(s3, W3S[og], b_wdram["w3"])
                ps1, bps1 = PS_POOL[s1], b_ps[s1]
                ps3, bps3 = PS_POOL[s3], b_ps[s3]
                mm_group(ps1, bps1, wbuf[s1], b_wbuf[s1], fin_t, b_fin)
                mm_group(ps3, bps3, wbuf[s3], b_wbuf[s3], fin_t, b_fin)
                hs = hids[:, (og % 2) * CL:(og % 2) * CL + CL]
                sc(lambda e, hs=hs, ps1=ps1: e.activation(hs, ps1[:],
                                                          AF.Silu),
                   [bps1], [b_hids[og % 2]])
                hg_ap = hid_t[:, og * CL:(og + 1) * CL]
                dv(lambda e, hs=hs, hg_ap=hg_ap, ps3=ps3: e.tensor_mul(
                    hg_ap, hs, ps3[:]),
                   [b_hids[og % 2], bps3], [b_hid[og]])
            for og in range(NG):
                pso, bpso = PS_POOL[og % 4], b_ps[og % 4]
                for q in range(4):
                    sl = (og * 4 + q) % 4
                    wsrc = W2S[og, :, q * 16:(q + 1) * 16, :]
                    wload(sl, wsrc, b_wdram["w2s"])
                    for kk in range(NG):
                        kf = q * 16 + kk
                        pe(lambda e, kf=kf, kk=kk, sl=sl, pso=pso: e.matmul(
                            pso[:], wbuf[sl][:, kk * 128:(kk + 1) * 128],
                            hid_t[:, kf * CL:(kf + 1) * CL],
                            start=(kf == 0), stop=(kf == NFG - 1)),
                           [b_wbuf[sl], b_hid[kf]],
                           [bpso] if kf in (0, NFG - 1) else [])
                fog = fout_t[:, og * CL:(og + 1) * CL]
                sc(lambda e, fog=fog, pso=pso: e.copy(fog, pso[:]),
                   [bpso], [b_fout[og]])
            fo_aps = [fout_t[:, g * CL:(g + 1) * CL] for g in range(NG)]
            pA5, pB5, bA5, bB5 = stat_round(fo_aps, b_fout, "p5", False,
                                            ["p5r", "p5rC"])
            for half in range(2):
                for lo in range(8):
                    og = half * 8 + lo
                    fog = fout_t[:, og * CL:(og + 1) * CL]
                    sg = s35_t[:, og * CL:(og + 1) * CL]
                    ot = outst[:, lo * CL:(lo + 1) * CL]
                    xg = xt[:, og * CL:(og + 1) * CL]
                    tm = tmp32[og % 2]
                    dv(lambda e, fog=fog, tm=tm, pA5=pA5: e.tensor_mul(
                        tm[:], fog, pA5[:]),
                       [b_fout[og], bA5], [b_tmp[og % 2]])
                    dv(lambda e, tm=tm, pB5=pB5: e.tensor_sub(
                        tm[:], tm[:], pB5[:]),
                       [b_tmp[og % 2], bB5], [b_tmp[og % 2]])
                    dv(lambda e, tm=tm, sg=sg: e.tensor_add(tm[:], tm[:],
                                                            sg),
                       [b_tmp[og % 2], b_s35[og]], [b_tmp[og % 2]])
                    dv(lambda e, tm=tm, ot=ot, xg=xg: e.tensor_add(
                        ot, tm[:], xg),
                       [b_tmp[og % 2], b_x[og]], [b_out[lo]])
                tok = None
                for lo in range(8):
                    og = half * 8 + lo
                    ot = outst[:, lo * CL:(lo + 1) * CL]
                    tok = P.op("sc", lambda e, b=b, og=og, sr=sr, ot=ot:
                               e.dma_start(
                                   out=OUT[b, og, :,
                                           sr * CL:(sr + 1) * CL],
                                   in_=ot),
                               [b_out[lo]], [b_io], dma=True, sem="ost")
                for lo in range(8):
                    b_out[lo].r = [tok]
            if nxt is not None:
                projscan(*nxt)
            return npair

        if nb == 4:
            order = [(0, 0), (0, 1), (0, 2), (0, 3), (1, 0), (0, 4), (1, 1),
                     (0, 5), (1, 2), (1, 3), (2, 0), (1, 4), (2, 1), (1, 5),
                     (2, 2), (2, 3), (3, 0), (2, 4), (3, 1), (2, 5), (3, 2),
                     (3, 3), (3, 4), (3, 5)]
        else:
            order = [(bb, ss) for bb in range(nb) for ss in range(NCH)]
        assert sorted(order) == sorted(
            (bb, ss) for bb in range(nb) for ss in range(NCH))
        for bb in range(nb):
            seq = [ss for (b2, ss) in order if b2 == bb]
            assert seq == sorted(seq)

        xpar = {t: i % 2 for i, t in enumerate(order)}
        emit_xload(*order[0], xpar[order[0]])
        fpair = front(*order[0], xpar[order[0]])
        xnmix(*order[0], xpar[order[0]], fpair)
        projscan(*order[0])
        done_a = {order[0]}
        ci = 0
        while ci < len(order):
            bb, ss = order[ci]
            if (bb, ss) not in done_a:
                emit_xload(bb, ss, xpar[(bb, ss)])
                fpair = front(bb, ss, xpar[(bb, ss)])
                xnmix(bb, ss, xpar[(bb, ss)], fpair)
                projscan(bb, ss)
                done_a.add((bb, ss))
                continue
            if ss < NWARM:
                ci += 1
                if do_fold and (bb, ss) == (0, 1):
                    emit_fold()
                continue
            nxt = order[ci + 1] if ci + 1 < len(order) else None
            if nxt is not None and nxt in done_a:
                nxt = None
            np_ = body(bb, ss, xpar[(bb, ss)],
                       xpar[nxt] if nxt is not None else 0, nxt)
            if nxt is not None:
                fpair = np_
                done_a.add(nxt)
            ci += 1

        P.finalize()
        dma_keys = [k for k in P.final_counts
                    if k not in ("pe", "dv", "sc", "gp")]
        dma_fin = [(k, P.final_counts[k]) for k in dma_keys]

        @block.sync
        def _(sy):
            P.emit("sy", sy, sems)

        @block.gpsimd
        def _(gpe):
            P.emit("gp", gpe, sems)
            for k, v in dma_fin:
                gpe.wait_ge(sems[k], v)

        @block.tensor
        def _(te):
            P.emit("pe", te, sems)

        @block.vector
        def _(ve):
            P.emit("dv", ve, sems)

        @block.scalar
        def _(se):
            P.emit("sc", se, sems)

    return nc


def _hadamard(m):
    h = np.array([[1.0]], np.float32)
    while h.shape[0] < m:
        h = np.block([[h, h], [h, -h]])
    return np.ascontiguousarray(h, np.float32)


def _prep_consts(inp):
    def tile_w(w):
        o, i = w.shape
        return np.ascontiguousarray(
            w.reshape(o // 128, 128, i // 128, 128).transpose(0, 3, 2, 1))

    def colv(v):
        return np.ascontiguousarray(v.reshape(NG, 128).T)

    return {
        "wkT": tile_w(inp["L0_attn_Wk"]), "wvT": tile_w(inp["L0_attn_Wv"]),
        "wrT": tile_w(inp["L0_attn_Wr"]), "woT": tile_w(inp["L0_attn_Wo"]),
        "w1T": tile_w(inp["L0_ffn_W1"]), "w3T": tile_w(inp["L0_ffn_W3"]),
        "w2A": np.ascontiguousarray(
            inp["L0_ffn_W2"].reshape(H, 64, 128).transpose(1, 2, 0)),
        "lnw": colv(inp["L0_attn_ln_w"]), "lnb": colv(inp["L0_attn_ln_b"]),
        "mk": colv(inp["L0_attn_mix_k"]), "mv": colv(inp["L0_attn_mix_v"]),
        "mr": colv(inp["L0_attn_mix_r"]),
        "td": colv(inp["L0_attn_time_decay"]),
        "uu": colv(inp["L0_attn_current_token_weight"]),
        "h64": _hadamard(64),
        "h128s": np.ascontiguousarray(_hadamard(128) / np.sqrt(8192.0),
                                      np.float32),
        "onesc": np.full((128, 1), 1.0 / 2048, np.float32),
        "onesr": np.ones((1, 128), np.float32),
    }


def kernel(**inp):
    x = np.ascontiguousarray(inp["L0_attn_hidden_state"])
    nb = 4
    consts = _prep_consts(inp)
    nc = build()
    in_maps = []
    for c in range(NCORE):
        if c == 0:
            xs = np.concatenate(
                [np.zeros((nb, WARM, H), np.float32), x[:, 0:TREAL]], axis=1)
        else:
            xs = x[:, c * TREAL - WARM:c * TREAL + TREAL]
        xs = np.ascontiguousarray(
            xs.reshape(nb, WARM + TREAL, NG, 128).transpose(0, 2, 3, 1))
        m = dict(consts)
        m["xs"] = xs
        m["kmask"] = np.full((128, 1), -1e9 if c == 0 else 0.0, np.float32)
        m["gmask"] = np.full((128, 1), 0.0 if c == 0 else 1.0, np.float32)
        in_maps.append(m)
    res = run_bass_kernel_spmd(nc, in_maps, core_ids=list(range(NCORE)))
    outs = []
    for c in range(NCORE):
        o = np.asarray(res.results[c]["out"])
        outs.append(np.ascontiguousarray(
            o.transpose(0, 3, 1, 2).reshape(nb, TREAL, H)))
    return np.concatenate(outs, axis=1)


# version 24
# speedup vs baseline: 1.7832x; 1.2697x over previous
import numpy as np
from contextlib import ExitStack

import concourse.bass as bass
import concourse.mybir as mybir
from concourse.bass_utils import run_bass_kernel_spmd

F32 = mybir.dt.float32
BF16 = mybir.dt.bfloat16
AF = mybir.ActivationFunctionType
ALU = mybir.AluOpType
AX = mybir.AxisListType

H = 2048
NG = 16
Fh = 8192
NFG = 64
L = 512
WARM = 512
TREAL = 1024
NCORE = 8


class Buf:
    __slots__ = ("w", "r", "name", "poison")

    def __init__(self, name=""):
        self.w = None
        self.r = []
        self.name = name
        self.poison = False


class Prog:
    def __init__(self):
        self.items = []
        self.cnt = {}
        self.hw = {}
        self.waited = set()

    def op(self, eng, fn, reads=(), writes=(), dma=False, sem=None):
        semkey = sem if sem else (eng + "d" if dma else eng)
        amt = 16 if dma else 1
        need = {}

        def add(tok):
            if tok is not None and need.get(tok[0], -1) < tok[1]:
                need[tok[0]] = tok[1]

        for b in reads:
            assert not b.poison, f"read of poisoned buffer {b.name}"
            add(b.w)
        for b in writes:
            add(b.w)
            for t in b.r:
                add(t)
            b.poison = False
        v = self.cnt.get(semkey, 0) + amt
        self.cnt[semkey] = v
        tok = (semkey, v)
        waits = []
        for sk, nv in sorted(need.items()):
            if eng == "pe" and sk == "pe":
                continue
            if self.hw.get((eng, sk), 0) >= nv:
                continue
            waits.append((sk, nv))
            self.hw[(eng, sk)] = nv
            self.waited.add((sk, nv))
        self.items.append([eng, fn, waits, semkey, amt, v])
        for b in reads:
            b.r.append(tok)
        for b in writes:
            b.w = tok
            b.r = []
        return tok

    def fence(self, out_bufs, in_bufs):
        toks = []
        for b in out_bufs:
            if b.w is not None:
                toks.append(b.w)
            toks.extend(b.r)
            b.poison = True
        for b in in_bufs:
            b.w = None
            b.r = list(toks)
            b.poison = False

    def finalize(self):
        keep, newv, cnt = {}, {}, {}
        for it in self.items:
            _, _, _, semkey, amt, v = it
            tk = (semkey, v)
            if amt == 16 or tk in self.waited:
                cnt[semkey] = cnt.get(semkey, 0) + amt
                keep[tk] = True
                newv[tk] = cnt[semkey]
            else:
                keep[tk] = False
        for it in self.items:
            it[2] = [(sk, newv[(sk, v)]) for sk, v in it[2]]
            it[5] = newv[(it[3], it[5])] if keep[(it[3], it[5])] else None
        self.final_counts = cnt

    def emit(self, engname, engobj, sems):
        for eng, fn, waits, semkey, amt, newval in self.items:
            if eng != engname:
                continue
            for sk, v in waits:
                engobj.wait_ge(sems[sk], v)
            inst = fn(engobj)
            if newval is not None:
                inst.then_inc(sems[semkey], amt)


STAT = {"mu": 0, "e2": 1, "v": 2, "t": 3, "t2": 4,
        "fA": 5, "fAC": 6, "fr": 7, "frC": 8,
        "p3r": 9, "p3rC": 10, "p4r": 9, "p4rC": 10,
        "p5r": 9, "p5rC": 10}


def build(nb=4, nch=3, do_fold=True):
    TL = nch * L
    TO = (nch - 1) * L
    nc = bass.Bass("TRN2", target_bir_lowering=False)

    def din(name, shape, dtype=F32):
        return nc.dram_tensor(name, shape, dtype, kind="ExternalInput").ap()

    XS = din("xs", [nb, NG, 128, TL])
    WKT = din("wkT", [NG, 128, NG, 128])
    WVT = din("wvT", [NG, 128, NG, 128])
    WRT = din("wrT", [NG, 128, NG, 128])
    WOT = din("woT", [NG, 128, NG, 128])
    W1T = din("w1T", [NFG, 128, NG, 128])
    W3T = din("w3T", [NFG, 128, NG, 128])
    W2A = din("w2A", [64, 128, H])
    LNW = din("lnw", [128, NG])
    LNB = din("lnb", [128, NG])
    MKc = din("mk", [128, NG])
    MVc = din("mv", [128, NG])
    MRc = din("mr", [128, NG])
    TDc = din("td", [128, NG])
    UUc = din("uu", [128, NG])
    KM = din("kmask", [128, 1])
    GM = din("gmask", [128, 1])
    H64 = din("h64", [64, 64])
    H128S = din("h128s", [128, 128])
    ONESC = din("onesc", [128, 1])
    ONESR = din("onesr", [1, 128])

    OUT = nc.dram_tensor("out", [nb, NG, 128, TO], F32,
                         kind="ExternalOutput").ap()

    WKS = nc.dram_tensor("wkS", [NG, 128, NG, 128], BF16).ap()
    WVS = nc.dram_tensor("wvS", [NG, 128, NG, 128], BF16).ap()
    WRS = nc.dram_tensor("wrS", [NG, 128, NG, 128], BF16).ap()
    WOS = nc.dram_tensor("woS", [NG, 128, NG, 128], BF16).ap()
    W1S = nc.dram_tensor("w1S", [NFG, 128, NG, 128], BF16).ap()
    W3S = nc.dram_tensor("w3S", [NFG, 128, NG, 128], BF16).ap()
    W2AB = nc.dram_tensor("w2Ab", [64, 128, H], BF16).ap()
    SCRA = nc.dram_tensor("scrA", [64, 128, H], BF16).ap()
    W2S = nc.dram_tensor("w2S", [NG, 128, NFG, 128], BF16).ap()

    P = Prog()

    with ExitStack() as es:
        sb_base = nc._sbuf_addr_for_side("left")
        sb_top = sb_base + nc.sbuf_bytes_remaining
        cur = [sb_base]

        def alloc(name, shape, dtype, at=None):
            per = int(np.prod(shape[1:])) * mybir.dt.size(dtype)
            if at is None:
                off = (cur[0] + 63) & ~63
                cur[0] = off + per
            else:
                off = at
            assert off + per <= sb_top, f"sbuf overflow {name} {off+per}"
            return nc.alloc_sbuf_tensor_at(name, shape, dtype,
                                           offset=off).ap(), off, per

        def A(name, shape, dtype, at=None):
            ap, off, per = alloc(name, shape, dtype, at)
            return ap

        x_t = A("x_t", [128, NG * 512], F32)
        sqb = A("sqb", [128, 2 * 512], BF16)
        strow = [A(f"strow{i}", [1, 512], F32) for i in range(11)]
        hirow = A("hirow", [1, 512], BF16)
        lorow = A("lorow", [1, 512], BF16)
        wbuf = [A(f"wbuf{i}", [128, NG * 128], BF16) for i in range(4)]
        outst = A("outst", [128, 2 * 512], F32)
        tmp32 = [A(f"tmp32_{i}", [128, 512], F32) for i in range(2)]
        dmix = A("dmix", [128, 512], BF16)
        lnw_t = A("lnw_t", [128, NG], F32)
        lnb_t = A("lnb_t", [128, NG], F32)
        mk_t = A("mk_t", [128, NG], F32)
        mv_t = A("mv_t", [128, NG], F32)
        mr_t = A("mr_t", [128, NG], F32)
        td_t = A("td_t", [128, NG], F32)
        uu_t = A("uu_t", [128, NG], F32)
        lam_t = A("lam_t", [128, NG], F32)
        etmp_t = A("etmp_t", [128, NG], F32)
        km_t = A("km_t", [128, 1], F32)
        gm_t = A("gm_t", [128, 1], F32)
        onesc_b = A("onesc_b", [128, 1], BF16)
        onesr_b = A("onesr_b", [1, 128], BF16)
        h64_b = A("h64_b", [64, 64], BF16)
        h128s_b = A("h128s_b", [128, 128], BF16)
        carr_m = A("carr_m", [128, 4 * NG], F32)
        carr_D = A("carr_D", [128, 4 * NG], F32)
        carr_E = A("carr_E", [128, 4 * NG], F32)
        scol = A("scol", [128, 8], F32)
        epsc = A("epsc", [1, 8], F32)
        guard_t = A("guard_t", [128, 4 * NG], BF16)

        xre = A("xre", [128, 2 * 512], F32)
        rbase = (cur[0] + 63) & ~63
        xn_t = A("xn_t", [128, NG * 513], BF16, at=rbase)
        xk_t = A("xk_t", [128, NG * 512], BF16, at=rbase + 16448)
        xv_t = A("xv_t", [128, NG * 512], BF16, at=rbase + 32832)
        xr_t = A("xr_t", [128, NG * 512], BF16, at=rbase + 49216)
        sc0 = rbase + 65600
        kq_t = A("kq_t", [128, 2 * 512], F32, at=sc0)
        vq_t = A("vq_t", [128, 2 * 512], F32, at=sc0 + 4096)
        p_t = A("p_t", [128, 512], F32, at=sc0 + 8192)
        pv_t = A("pv_t", [128, 512], F32, at=sc0 + 10240)
        bon_t = A("bon_t", [128, 512], F32, at=sc0 + 12288)
        D_t = A("D_t", [128, 513], F32, at=sc0 + 14336)
        E_t = A("E_t", [128, 513], F32, at=sc0 + 16448)
        r_t = A("r_t", [128, NG * 512], BF16, at=sc0 + 18560)
        xbf_t = A("xbf_t", [128, NG * 512], BF16, at=sc0 + 18560 + 16384)
        attn_end = sc0 + 18560 + 32768
        ao_t = A("ao_t", [128, NG * 512], BF16, at=rbase)
        part_t = A("part_t", [128, NG * 512], BF16, at=rbase)
        s35_t = A("s35_t", [128, NG * 512], BF16, at=rbase + 16384)
        x2b_t = A("x2b_t", [128, NG * 512], BF16, at=rbase + 32768)
        fin_t = A("fin_t", [128, NG * 512], BF16, at=rbase + 49152)
        fout_t = A("fout_t", [128, NG * 512], BF16, at=rbase + 49152)
        hid_t = A("hid_t", [128, 32 * 512], BF16, at=rbase + 65536)
        hids = A("hids", [128, 2 * 512], BF16, at=rbase + 98304)
        fA = A("fA", [128, 2048], BF16, at=rbase + 49216)
        fB = A("fB", [128, NG * 128], BF16, at=rbase + 49216 + 4096)
        fE = A("fE", [128, 2 * 2048], BF16, at=rbase + 49216 + 8192)
        region_end = attn_end
        assert region_end <= sb_top, f"sbuf over budget: {region_end}>{sb_top}"

        ps = [es.enter_context(nc.psum_tensor(f"ps{i}", [128, 512], F32))
              for i in range(8)]
        PS_POOL = ps[0:4]
        rot = [0]

        def pair(peek=False):
            i = rot[0] % 2
            if not peek:
                rot[0] += 1
            return ps[4 + i * 2], ps[5 + i * 2], b_ps[4 + i * 2], b_ps[5 + i * 2]

        semkeys = (["pe", "dv", "sc", "gp", "gpd", "xl", "ost0", "ost1", "xr0",
                    "xr1", "fa", "fb", "fs0", "fs1", "ws"] +
                   [f"w{i}" for i in range(4)] +
                   [f"c{i}" for i in range(8)])
        sems = {k: es.enter_context(nc.semaphore(f"s_{k}")) for k in semkeys}
        block = es.enter_context(nc.Block())

        def bufs(pref, n):
            return [Buf(f"{pref}{i}") for i in range(n)]

        b_x = bufs("x", NG)
        b_xbf = bufs("xbf", NG)
        b_xn = bufs("xn", NG)
        b_xk = bufs("xk", NG)
        b_xv = bufs("xv", NG)
        b_xr = bufs("xr", NG)
        b_kq = bufs("kq", 2)
        b_vq = bufs("vq", 2)
        b_p, b_pv, b_bon = Buf("p"), Buf("pv"), Buf("bon")
        b_D, b_E = Buf("D"), Buf("E")
        b_r = bufs("r", NG)
        b_ao = bufs("ao", NG)
        b_s35 = bufs("s35", NG)
        b_x2b = bufs("x2b", NG)
        b_fin = bufs("fin", NG)
        b_fout = bufs("fout", NG)
        b_hid = bufs("hid", 32)
        b_part = bufs("part", NG)
        b_hids = bufs("hids", 2)
        b_sq = bufs("sq", 2)
        b_wbuf = bufs("wbuf", 4)
        b_out = bufs("outst", 2)
        b_tmp = bufs("tmp32_", 2)
        b_dmix = Buf("dmix")
        b_ps = [Buf(f"ps{i}") for i in range(8)]
        b_row = [Buf(f"row{i}") for i in range(12)]
        b_hi, b_lo = Buf("hirow"), Buf("lorow")
        b_carr = bufs("carr", 4 * NG)
        b_scol = Buf("scol")
        b_consts = Buf("consts")
        b_guard = bufs("guard", 4)
        b_fA = Buf("fA")
        b_fB = Buf("fB")
        b_fE = bufs("fE", 4)
        b_scrA = Buf("scrA")
        b_scrA2 = Buf("scrA2")
        b_wdram = {n: Buf("wd_" + n) for n in
                   ["wk", "wv", "wr", "wo", "w1", "w3", "w2ab", "w2s"]}
        b_io = Buf("io")
        b_xre = bufs("xre", 2)

        ATTN_B = (b_xk + b_xv + b_xr + b_kq + b_vq +
                  [b_p, b_pv, b_bon, b_D, b_E] + b_r)
        ATTN_ALL = b_xn + ATTN_B
        POST_SET = b_ao
        FFN_SET = b_s35 + b_x2b + b_fin + b_hid + b_part + b_hids
        FOLD_SET = [b_fA, b_fB] + b_fE

        def row(i):
            return strow[i][:]

        def dv(fn, reads, writes):
            return P.op("dv", fn, reads, writes)

        def sc(fn, reads, writes):
            return P.op("sc", fn, reads, writes)

        def pe(fn, reads, writes):
            return P.op("pe", fn, reads, writes)

        def gpd(fn, reads, writes, sem=None):
            return P.op("gp", fn, reads, writes, dma=True, sem=sem)

        def syd(fn, reads, writes, sem):
            return P.op("sy", fn, reads, writes, dma=True, sem=sem)

        for src, dst in ((LNW, lnw_t), (LNB, lnb_t), (MKc, mk_t),
                         (MVc, mv_t), (MRc, mr_t), (TDc, td_t), (UUc, uu_t),
                         (KM, km_t), (GM, gm_t), (H64, h64_b),
                         (H128S, h128s_b), (ONESC, onesc_b),
                         (ONESR, onesr_b)):
            gpd(lambda e, s=src, d=dst: e.dma_start(out=d[:], in_=s),
                [], [b_consts])
        for i, v in enumerate((1e-10, 1e-5, 1.00001, -0.5)):
            dv(lambda e, i=i, v=v: e.memset(epsc[0:1, i:i + 1], v),
               [], [b_consts])
        sc(lambda e: e.activation(etmp_t[:], td_t[:], AF.Exp),
           [b_consts], [b_consts])
        sc(lambda e: e.activation(lam_t[:], etmp_t[:], AF.Exp, scale=-1.0),
           [b_consts], [b_consts])

        castn = [0]

        def cast_w(src, dst, bname):
            gpd(lambda e, s=src, d=dst: e.dma_start(out=d, in_=s),
                [], [b_wdram[bname]], sem=f"c{castn[0]}")
            castn[0] += 1

        cast_w(WKT, WKS, "wk")
        cast_w(WVT, WVS, "wv")
        for _src, _dst, _nm in [(WRT, WRS, "wr"), (WOT, WOS, "wo"),
                                (W2A, W2AB, "w2ab"), (W1T, W1S, "w1"),
                                (W3T, W3S, "w3")]:
            cast_w(_src, _dst, _nm)

        def bcast(ri, psd, psb):
            rap = row(ri)
            rb = b_row[ri]
            dv(lambda e: e.tensor_copy(hirow[:], rap), [rb], [b_hi])
            dv(lambda e: e.tensor_sub(lorow[:], rap, hirow[:]),
               [rb, b_hi], [b_lo])
            pe(lambda e: e.matmul(psd[:], onesr_b[:], hirow[:],
                                  start=True, stop=False),
               [b_consts, b_hi], [psb])
            pe(lambda e: e.matmul(psd[:], onesr_b[:], lorow[:],
                                  start=False, stop=True),
               [b_consts, b_lo], [psb])

        def stats(aps, bbs, pA, pB, bA, bB):
            for g in range(NG):
                sq = sqb[:, (g % 2) * 512:(g % 2) * 512 + 512]
                sc(lambda e, g=g, sq=sq: e.activation(sq, aps[g], AF.Square),
                   [bbs[g]], [b_sq[g % 2]])
                pe(lambda e, g=g, pA=pA: e.matmul(
                    pA[0:1, :], onesc_b[:], aps[g],
                    start=(g == 0), stop=(g == NG - 1)),
                   [b_consts, bbs[g]],
                   [bA] if g in (0, NG - 1) else [])
                pe(lambda e, g=g, sq=sq, pB=pB: e.matmul(
                    pB[0:1, :], onesc_b[:], sq,
                    start=(g == 0), stop=(g == NG - 1)),
                   [b_consts, b_sq[g % 2]],
                   [bB] if g in (0, NG - 1) else [])

        def stat_math(pref, inner, pA, pB, bA, bB):
            iu, iv = STAT["mu"], STAT["v"]
            it, it2 = STAT["t"], STAT["t2"]
            dv(lambda e, pA=pA: e.tensor_copy(row(iu), pA[0:1, :]),
               [bA], [b_row[iu]])
            dv(lambda e: e.tensor_mul(row(it), row(iu), row(iu)),
               [b_row[iu]], [b_row[it]])
            dv(lambda e, pB=pB: e.tensor_sub(row(iv), pB[0:1, :], row(it)),
               [bB, b_row[it]], [b_row[iv]])
            specs = ([(pref + "A", 1.00001, 1e-10)] if inner else []) + \
                [(pref + "r", 1.0, 1e-5)]
            for nm, aa, bb in specs:
                io, ic = STAT[nm], STAT[nm + "C"]
                bia = epsc[0:1, 0:1] if bb == 1e-10 else epsc[0:1, 1:2]
                sca = epsc[0:1, 2:3] if aa != 1.0 else 1.0
                sc(lambda e, bia=bia, sca=sca: e.activation(
                    row(it2), row(iv), AF.Ln, bias=bia, scale=sca),
                   [b_row[iv], b_consts], [b_row[it2]])
                sc(lambda e, io=io: e.activation(
                    row(io), row(it2), AF.Exp, scale=epsc[0:1, 3:4]),
                   [b_row[it2], b_consts], [b_row[io]])
                dv(lambda e, io=io, ic=ic: e.tensor_mul(row(ic), row(iu),
                                                        row(io)),
                   [b_row[iu], b_row[io]], [b_row[ic]])

        def stat_round(aps, bbs, pref, inner, bnames):
            pA, pB, bA, bB = pair()
            stats(aps, bbs, pA, pB, bA, bB)
            stat_math(pref, inner, pA, pB, bA, bB)
            bcast(STAT[bnames[0]], pA, bA)
            bcast(STAT[bnames[1]], pB, bB)
            return pA, pB, bA, bB

        def emit_fold():
            for pp in range(128):
                sl = pp % 2
                fa = fA[0:64, :]
                fe = fE[0:64, sl * 2048:(sl + 1) * 2048]
                syd(lambda e, pp=pp, fa=fa: e.dma_start(
                    out=fa, in_=W2AB[:, pp, :]),
                    [b_wdram["w2ab"]], [b_fA], sem="fa")
                for d4 in range(4):
                    feb = b_fE[sl * 2 + d4 % 2]
                    pe(lambda e, fa=fa, d4=d4: e.matmul(
                        ps[d4][0:64, :], h64_b[:],
                        fa[:, d4 * 512:(d4 + 1) * 512],
                        start=True, stop=True),
                       [b_consts, b_fA], [b_ps[d4]])
                    if d4 % 2:
                        sc(lambda e, fe=fe, d4=d4: e.copy(
                            fe[:, d4 * 512:(d4 + 1) * 512],
                            ps[d4][0:64, :]), [b_ps[d4]], [feb])
                    else:
                        dv(lambda e, fe=fe, d4=d4: e.tensor_copy(
                            fe[:, d4 * 512:(d4 + 1) * 512],
                            ps[d4][0:64, :]), [b_ps[d4]], [feb])
                syd(lambda e, pp=pp, fe=fe, sl=sl: e.dma_start(
                    out=SCRA[:, pp, :], in_=fe),
                    [b_fE[sl * 2], b_fE[sl * 2 + 1]],
                    [b_scrA if sl == 0 else b_scrA2], sem=f"fs{sl}")
            for g in range(64):
                syd(lambda e, g=g: e.dma_start(out=fB[:], in_=SCRA[g]),
                    [b_scrA, b_scrA2], [b_fB], sem="fb")
                for d4 in range(4):
                    sl = d4 % 2
                    pe(lambda e, d4=d4, sl=sl: e.matmul(
                        ps[sl][:], h128s_b[:],
                        fB[:, d4 * 512:d4 * 512 + 512],
                        start=True, stop=True),
                       [b_consts, b_fB], [b_ps[sl]])
                    fe = fE[:, d4 * 512:(d4 + 1) * 512]
                    sc(lambda e, fe=fe, sl=sl: e.copy(
                        fe, ps[sl][:]), [b_ps[sl]], [b_fE[d4]])

                    def store(e, g=g, d4=d4, fe=fe):
                        dstv = W2S.rearrange(
                            "ogd p kf m -> p kf ogd m")[:, g,
                                                        d4 * 4:(d4 + 1) * 4]
                        srcv = fe.rearrange("p (ogd m) -> p ogd m", m=128)
                        return e.dma_start(out=dstv, in_=srcv)
                    syd(store, [b_fE[d4]], [b_wdram["w2s"]], sem="ws")

        def mm_group(psd, psb, w_ap, wslot, rhs_t, rhs_bufs, n=NG):
            for kk in range(n):
                pe(lambda e, kk=kk: e.matmul(
                    psd[:], w_ap[:, kk * 128:(kk + 1) * 128],
                    rhs_t[:, kk * 512:(kk + 1) * 512],
                    start=(kk == 0), stop=(kk == n - 1)),
                   [wslot, rhs_bufs[kk]],
                   [psb] if kk in (0, n - 1) else [])

        def wload(sl, src, src_buf):
            P.op("sy", lambda e: e.dma_start(
                out=wbuf[sl][:].rearrange("p (k m) -> p k m", m=128),
                in_=src),
                [src_buf], [b_wbuf[sl]], dma=True, sem=f"w{sl}")

        def emit_xload(b, s):
            t0 = s * L
            tok = None
            for g in range(NG):
                tok = P.op("sc", lambda e, b=b, g=g, t0=t0: e.dma_start(
                    out=x_t[:, g * 512:(g + 1) * 512],
                    in_=XS[b, g, :, t0:t0 + L]),
                    [], [b_x[g]], dma=True, sem="xl")
            for g in range(NG):
                b_x[g].w = tok

        def front(b, s):
            for g in range(NG):
                P.op("gp", lambda e, g=g: e.tensor_copy(
                    xbf_t[:, g * 512:(g + 1) * 512],
                    x_t[:, g * 512:(g + 1) * 512]), [b_x[g]], [b_xbf[g]])
            xbf_aps = [xbf_t[:, g * 512:(g + 1) * 512] for g in range(NG)]
            return stat_round(xbf_aps, b_xbf, "f", True, ["fA", "fAC"])

        def chunk(b, s, nxt, fpair):
            warm = (s == 0)
            pA, pB, bA, bB = fpair

            def tail():
                if nxt is None:
                    return None
                emit_xload(*nxt)
                return front(*nxt)

            cb = b * NG
            bslc = slice(cb, cb + NG)
            if s == 0:
                dv(lambda e, bslc=bslc: e.memset(carr_m[:, bslc], -1e30),
                   [], b_carr[cb:cb + NG])
                dv(lambda e, bslc=bslc: e.memset(carr_D[:, bslc], 0.0),
                   [], b_carr[cb:cb + NG])
                dv(lambda e, bslc=bslc: e.memset(carr_E[:, bslc], 0.0),
                   [], b_carr[cb:cb + NG])
                dv(lambda e, bslc=bslc: e.memset(guard_t[:, bslc], 0.0),
                   [], [b_guard[b]])
            dv(lambda e, bslc=bslc: e.tensor_copy(
                xn_t[:].rearrange("p (g c) -> p g c", c=513)[:, :, 0],
                guard_t[:, bslc]), [b_guard[b]], b_xn)
            for g in range(NG):
                xng = xn_t[:, g * 513 + 1:g * 513 + 513]
                sxg = xn_t[:, g * 513:g * 513 + 512]
                xg0 = x_t[:, g * 512:(g + 1) * 512]
                tm = tmp32[g % 2]
                dv(lambda e, xg0=xg0, tm=tm, pA=pA: e.tensor_mul(
                    tm[:], xg0, pA[:]),
                   [b_x[g], bA], [b_tmp[g % 2]])
                dv(lambda e, tm=tm, pB=pB: e.tensor_sub(tm[:], tm[:],
                                                        pB[:]),
                   [b_tmp[g % 2], bB], [b_tmp[g % 2]])
                dv(lambda e, g=g, xng=xng, tm=tm: e.tensor_scalar(
                    xng, tm[:], lnw_t[:, g:g + 1], lnb_t[:, g:g + 1],
                    ALU.mult, ALU.add),
                   [b_tmp[g % 2], b_consts], [b_xn[g]])
                dv(lambda e, xng=xng, sxg=sxg: e.tensor_sub(dmix[:], xng,
                                                            sxg),
                   [b_xn[g]], [b_dmix])
                mixes = [(mk_t, xk_t, b_xk), (mv_t, xv_t, b_xv)]
                if not warm:
                    mixes.append((mr_t, xr_t, b_xr))
                for mt, dstt, bb2 in mixes:
                    dv(lambda e, g=g, mt=mt, dstt=dstt, sxg=sxg:
                       e.scalar_tensor_tensor(
                           dstt[:, g * 512:(g + 1) * 512], dmix[:],
                           mt[:, g:g + 1], sxg, ALU.mult, ALU.add),
                       [b_dmix, b_xn[g], b_consts], [bb2[g]])
            dv(lambda e, bslc=bslc: e.tensor_copy(
                guard_t[:, bslc],
                xn_t[:].rearrange("p (g c) -> p g c", c=513)[:, :, 512]),
               b_xn, [b_guard[b]])

            def proj_emit(hg):
                par = hg % 2
                sk, sv = (hg * 2) % 4, (hg * 2 + 1) % 4
                wload(sk, WKS[hg], b_wdram["wk"])
                wload(sv, WVS[hg], b_wdram["wv"])
                psk, bpsk = PS_POOL[2 * par], b_ps[2 * par]
                psv, bpsv = PS_POOL[2 * par + 1], b_ps[2 * par + 1]
                mm_group(psk, bpsk, wbuf[sk], b_wbuf[sk], xk_t, b_xk)
                mm_group(psv, bpsv, wbuf[sv], b_wbuf[sv], xv_t, b_xv)
                kq = kq_t[:, par * 512:par * 512 + 512]
                vq = vq_t[:, par * 512:par * 512 + 512]
                sc(lambda e, kq=kq, psk=psk: e.copy(kq, psk[:]),
                   [bpsk], [b_kq[par]])
                sc(lambda e, vq=vq, psv=psv: e.copy(vq, psv[:]),
                   [bpsv], [b_vq[par]])
                if not warm:
                    wload(sk, WRS[hg], b_wdram["wr"])
                    rA, rB, rbA, rbB = pair(peek=True)
                    psr, bpsr = (rA, rbA) if par == 0 else (rB, rbB)
                    mm_group(psr, bpsr, wbuf[sk], b_wbuf[sk], xr_t, b_xr)
                    rg = r_t[:, hg * 512:(hg + 1) * 512]
                    sc(lambda e, rg=rg, psr=psr: e.activation(
                        rg, psr[:], AF.Sigmoid), [bpsr], [b_r[hg]])

            def scan_emit(hg):
                par = hg % 2
                kq = kq_t[:, par * 512:par * 512 + 512]
                vq = vq_t[:, par * 512:par * 512 + 512]
                lm, mc, negm = scol[:, 0:1], scol[:, 1:2], scol[:, 2:3]
                ub, cs = scol[:, 3:4], scol[:, 4:5]
                mp = carr_m[:, cb + hg:cb + hg + 1]
                Dc = carr_D[:, cb + hg:cb + hg + 1]
                Ec = carr_E[:, cb + hg:cb + hg + 1]
                dv(lambda e, kq=kq: e.reduce_max(lm, kq, axis=AX.X),
                   [b_kq[par]], [b_scol])
                if warm:
                    dv(lambda e: e.tensor_add(lm, lm, km_t[:]),
                       [b_scol, b_consts], [b_scol])
                dv(lambda e, mp=mp: e.tensor_max(mc, lm, mp),
                   [b_scol, b_carr[cb + hg]], [b_scol])
                dv(lambda e: e.tensor_scalar_mul(negm, mc, -1.0),
                   [b_scol], [b_scol])
                pb = scol[:, 5:6]
                if warm:
                    dv(lambda e: e.tensor_add(pb, negm, km_t[:]),
                       [b_scol, b_consts], [b_scol])
                else:
                    pb = negm
                sc(lambda e, kq=kq, pb=pb: e.activation(p_t[:], kq, AF.Exp,
                                                        bias=pb),
                   [b_kq[par], b_scol], [b_p])
                dv(lambda e, vq=vq: e.tensor_mul(pv_t[:], p_t[:], vq),
                   [b_p, b_vq[par]], [b_pv])
                sc(lambda e, mp=mp: e.activation(cs, mp, AF.Exp, bias=negm),
                   [b_carr[cb + hg], b_scol], [b_scol])
                dv(lambda e, Dc=Dc: e.tensor_mul(D_t[:, 0:1], Dc, cs),
                   [b_carr[cb + hg], b_scol], [b_D])
                dv(lambda e, Ec=Ec: e.tensor_mul(E_t[:, 0:1], Ec, cs),
                   [b_carr[cb + hg], b_scol], [b_E])
                lamb = lam_t[:, hg:hg + 1].broadcast_to([128, 512])
                dv(lambda e, lamb=lamb: e.tensor_tensor_scan(
                    D_t[:, 1:513], lamb, pv_t[:], D_t[:, 0:1],
                    ALU.mult, ALU.add),
                   [b_consts, b_pv, b_D], [b_D])
                dv(lambda e, lamb=lamb: e.tensor_tensor_scan(
                    E_t[:, 1:513], lamb, p_t[:], E_t[:, 0:1],
                    ALU.mult, ALU.add),
                   [b_consts, b_p, b_E], [b_E])
                dv(lambda e, Dc=Dc: e.tensor_copy(Dc, D_t[:, 512:513]),
                   [b_D], [b_carr[cb + hg]])
                dv(lambda e, Ec=Ec: e.tensor_copy(Ec, E_t[:, 512:513]),
                   [b_E], [b_carr[cb + hg]])
                dv(lambda e, mp=mp: e.tensor_copy(mp, mc),
                   [b_scol], [b_carr[cb + hg]])
                if not warm:
                    dv(lambda e, hg=hg: e.tensor_add(
                        ub, uu_t[:, hg:hg + 1], negm),
                       [b_consts, b_scol], [b_scol])
                    sc(lambda e, kq=kq: e.activation(bon_t[:], kq, AF.Exp,
                                                     bias=ub),
                       [b_kq[par], b_scol], [b_bon])
                    dv(lambda e, vq=vq: e.tensor_mul(pv_t[:], bon_t[:], vq),
                       [b_bon, b_vq[par], b_pv], [b_pv])
                    dv(lambda e: e.tensor_add(pv_t[:], pv_t[:],
                                              D_t[:, 0:512]),
                       [b_pv, b_D], [b_pv])
                    dv(lambda e: e.tensor_add(p_t[:], bon_t[:],
                                              E_t[:, 0:512]),
                       [b_bon, b_E, b_p], [b_p])
                    sc(lambda e: e.activation(bon_t[:], p_t[:], AF.Ln),
                       [b_p], [b_bon])
                    sc(lambda e: e.activation(bon_t[:], bon_t[:], AF.Exp,
                                              scale=-1.0),
                       [b_bon], [b_bon])
                    dv(lambda e: e.tensor_mul(p_t[:], pv_t[:], bon_t[:]),
                       [b_pv, b_bon, b_p], [b_p])
                    rg = r_t[:, hg * 512:(hg + 1) * 512]
                    dv(lambda e, rg=rg: e.tensor_mul(rg, rg, p_t[:]),
                       [b_r[hg], b_p], [b_r[hg]])

            for hg in range(NG):
                proj_emit(hg)
                if hg > 0:
                    scan_emit(hg - 1)
            scan_emit(NG - 1)
            if warm:
                return tail()

            pA1, pB1, bA1, bB1 = pair()
            bcast(STAT["fr"], pA1, bA1)
            bcast(STAT["frC"], pB1, bB1)
            P.fence(b_xn, POST_SET)
            pA3, pB3, bA3, bB3 = pair()
            for og in range(NG):
                sl = og % 4
                wload(sl, WOS[og], b_wdram["wo"])
                pso, bpso = PS_POOL[sl], b_ps[sl]
                mm_group(pso, bpso, wbuf[sl], b_wbuf[sl], r_t, b_r)
                xg = x_t[:, og * 512:(og + 1) * 512]
                aog = ao_t[:, og * 512:(og + 1) * 512]
                tm = tmp32[og % 2]
                dv(lambda e, xg=xg, tm=tm, pA1=pA1: e.tensor_mul(
                    tm[:], xg, pA1[:]),
                   [b_x[og], bA1], [b_tmp[og % 2]])
                dv(lambda e, tm=tm, pso=pso: e.tensor_add(tm[:], tm[:],
                                                          pso[:]),
                   [b_tmp[og % 2], bpso], [b_tmp[og % 2]])
                dv(lambda e, tm=tm, aog=aog, pB1=pB1: e.tensor_sub(
                    aog, tm[:], pB1[:]),
                   [b_tmp[og % 2], bB1], [b_ao[og]])
                sq = sqb[:, (og % 2) * 512:(og % 2) * 512 + 512]
                sc(lambda e, sq=sq, aog=aog: e.activation(sq, aog,
                                                          AF.Square),
                   [b_ao[og]], [b_sq[og % 2]])
                pe(lambda e, og=og, aog=aog, pA3=pA3: e.matmul(
                    pA3[0:1, :], onesc_b[:], aog,
                    start=(og == 0), stop=(og == NG - 1)),
                   [b_consts, b_ao[og]],
                   [bA3] if og in (0, NG - 1) else [])
                pe(lambda e, og=og, sq=sq, pB3=pB3: e.matmul(
                    pB3[0:1, :], onesc_b[:], sq,
                    start=(og == 0), stop=(og == NG - 1)),
                   [b_consts, b_sq[og % 2]],
                   [bB3] if og in (0, NG - 1) else [])
            stat_math("p3", False, pA3, pB3, bA3, bB3)
            bcast(STAT["p3r"], pA3, bA3)
            bcast(STAT["p3rC"], pB3, bB3)
            P.fence(ATTN_B, FFN_SET)
            for og in range(NG):
                aog = ao_t[:, og * 512:(og + 1) * 512]
                sg = s35_t[:, og * 512:(og + 1) * 512]
                xg = x_t[:, og * 512:(og + 1) * 512]
                x2g = x2b_t[:, og * 512:(og + 1) * 512]
                tm = tmp32[og % 2]
                dv(lambda e, aog=aog, tm=tm, pA3=pA3: e.tensor_mul(
                    tm[:], aog, pA3[:]),
                   [b_ao[og], bA3], [b_tmp[og % 2]])
                dv(lambda e, tm=tm, sg=sg, pB3=pB3: e.tensor_sub(
                    sg, tm[:], pB3[:]),
                   [b_tmp[og % 2], bB3], [b_s35[og]])
                dv(lambda e, xg=xg, sg=sg, x2g=x2g: e.tensor_add(x2g, xg,
                                                                 sg),
                   [b_x[og], b_s35[og]], [b_x2b[og]])
            P.fence(b_ao, b_part)
            x2_aps = [x2b_t[:, g * 512:(g + 1) * 512] for g in range(NG)]
            pA4, pB4, bA4, bB4 = stat_round(x2_aps, b_x2b, "p4", False,
                                            ["p4r", "p4rC"])
            for og in range(NG):
                x2g = x2b_t[:, og * 512:(og + 1) * 512]
                fg = fin_t[:, og * 512:(og + 1) * 512]
                tm = tmp32[og % 2]
                dv(lambda e, x2g=x2g, tm=tm, pA4=pA4: e.tensor_mul(
                    tm[:], x2g, pA4[:]),
                   [b_x2b[og], bA4], [b_tmp[og % 2]])
                dv(lambda e, tm=tm, fg=fg, pB4=pB4: e.tensor_sub(
                    fg, tm[:], pB4[:]),
                   [b_tmp[og % 2], bB4], [b_fin[og]])
            for half in range(2):
                if half == 1:
                    P.fence(b_hid, b_hid)
                for lo in range(32):
                    og = half * 32 + lo
                    s1, s3 = (lo * 2) % 4, (lo * 2 + 1) % 4
                    wload(s1, W1S[og], b_wdram["w1"])
                    wload(s3, W3S[og], b_wdram["w3"])
                    ps1, bps1 = PS_POOL[s1], b_ps[s1]
                    ps3, bps3 = PS_POOL[s3], b_ps[s3]
                    mm_group(ps1, bps1, wbuf[s1], b_wbuf[s1], fin_t, b_fin)
                    mm_group(ps3, bps3, wbuf[s3], b_wbuf[s3], fin_t, b_fin)
                    hs = hids[:, (lo % 2) * 512:(lo % 2) * 512 + 512]
                    sc(lambda e, hs=hs, ps1=ps1: e.activation(hs, ps1[:],
                                                              AF.Silu),
                       [bps1], [b_hids[lo % 2]])
                    hg_ap = hid_t[:, lo * 512:(lo + 1) * 512]
                    dv(lambda e, hs=hs, hg_ap=hg_ap, ps3=ps3: e.tensor_mul(
                        hg_ap, hs, ps3[:]),
                       [b_hids[lo % 2], bps3], [b_hid[lo]])
                if half == 0:
                    emit_xload(*nxt) if nxt is not None else None
                if half == 1:
                    P.fence(b_fin, b_fout)
                for og in range(NG):
                    pso, bpso = PS_POOL[og % 4], b_ps[og % 4]
                    for q in range(2):
                        sl = (og * 2 + q) % 4
                        qq = half * 2 + q
                        wsrc = W2S[og, :, qq * 16:(qq + 1) * 16, :]
                        wload(sl, wsrc, b_wdram["w2s"])
                        for kk in range(NG):
                            kl = q * 16 + kk
                            pe(lambda e, kl=kl, kk=kk, sl=sl, pso=pso:
                               e.matmul(
                                   pso[:],
                                   wbuf[sl][:, kk * 128:(kk + 1) * 128],
                                   hid_t[:, kl * 512:(kl + 1) * 512],
                                   start=(kl == 0), stop=(kl == 31)),
                               [b_wbuf[sl], b_hid[kl]],
                               [bpso] if kl in (0, 31) else [])
                    pg = part_t[:, og * 512:(og + 1) * 512]
                    if half == 0:
                        sc(lambda e, pg=pg, pso=pso: e.copy(pg, pso[:]),
                           [bpso], [b_part[og]])
                    else:
                        fog = fout_t[:, og * 512:(og + 1) * 512]
                        dv(lambda e, fog=fog, pg=pg, pso=pso: e.tensor_add(
                            fog, pso[:], pg),
                           [bpso, b_part[og]], [b_fout[og]])
            fo_aps = [fout_t[:, g * 512:(g + 1) * 512] for g in range(NG)]
            pA5, pB5, bA5, bB5 = stat_round(fo_aps, b_fout, "p5", False,
                                            ["p5r", "p5rC"])
            npair = front(*nxt) if nxt is not None else None
            for og in range(NG):
                fog = fout_t[:, og * 512:(og + 1) * 512]
                sg = s35_t[:, og * 512:(og + 1) * 512]
                ot = outst[:, (og % 2) * 512:(og % 2) * 512 + 512]
                xr_ap = xre[:, (og % 2) * 512:(og % 2) * 512 + 512]
                tm = tmp32[og % 2]
                P.op("sc", lambda e, b=b, og=og, s=s, xr_ap=xr_ap:
                     e.dma_start(
                         out=xr_ap, in_=XS[b, og, :, s * L:(s + 1) * L]),
                     [], [b_xre[og % 2]], dma=True, sem=f"xr{og % 2}")
                dv(lambda e, fog=fog, tm=tm, pA5=pA5: e.tensor_mul(
                    tm[:], fog, pA5[:]),
                   [b_fout[og], bA5], [b_tmp[og % 2]])
                dv(lambda e, tm=tm, pB5=pB5: e.tensor_sub(tm[:], tm[:],
                                                          pB5[:]),
                   [b_tmp[og % 2], bB5], [b_tmp[og % 2]])
                dv(lambda e, tm=tm, sg=sg: e.tensor_add(tm[:], tm[:], sg),
                   [b_tmp[og % 2], b_s35[og]], [b_tmp[og % 2]])
                dv(lambda e, tm=tm, ot=ot, xr_ap=xr_ap: e.tensor_add(
                    ot, tm[:], xr_ap),
                   [b_tmp[og % 2], b_xre[og % 2]], [b_out[og % 2]])
                P.op("sc", lambda e, b=b, og=og, s=s, ot=ot: e.dma_start(
                    out=OUT[b, og, :, (s - 1) * L:s * L], in_=ot),
                    [b_out[og % 2]], [b_io], dma=True, sem=f"ost{og % 2}")
            P.fence(POST_SET + FFN_SET, ATTN_ALL)
            return npair

        if nb == 4 and nch == 3:
            order = [(0, 0), (1, 0), (0, 1), (0, 2), (2, 0), (1, 1),
                     (1, 2), (3, 0), (2, 1), (2, 2), (3, 1), (3, 2)]
            foldat = 1
        else:
            order = [(bb, 0) for bb in range(nb)] + \
                [(bb, ss) for bb in range(nb) for ss in range(1, nch)]
            foldat = nb - 1
        emit_xload(*order[0])
        fpair = front(*order[0])
        for ci, (bb, ss) in enumerate(order):
            nxt = order[ci + 1] if ci + 1 < len(order) else None
            fpair = chunk(bb, ss, nxt, fpair)
            if ci == foldat and do_fold:
                emit_fold()
                P.fence(FOLD_SET, b_xr + b_kq + b_vq)

        P.finalize()
        dma_keys = [k for k in P.final_counts
                    if k not in ("pe", "dv", "sc", "gp")]
        dma_fin = [(k, P.final_counts[k]) for k in dma_keys]

        @block.sync
        def _(sy):
            P.emit("sy", sy, sems)

        @block.gpsimd
        def _(gp):
            P.emit("gp", gp, sems)
            for k, v in dma_fin:
                gp.wait_ge(sems[k], v)

        @block.tensor
        def _(te):
            P.emit("pe", te, sems)

        @block.vector
        def _(ve):
            P.emit("dv", ve, sems)

        @block.scalar
        def _(se):
            P.emit("sc", se, sems)

    return nc


def _hadamard(m):
    h = np.array([[1.0]], np.float32)
    while h.shape[0] < m:
        h = np.block([[h, h], [h, -h]])
    return np.ascontiguousarray(h, np.float32)


def _prep_consts(inp):
    def tile_w(w):
        o, i = w.shape
        return np.ascontiguousarray(
            w.reshape(o // 128, 128, i // 128, 128).transpose(0, 3, 2, 1))

    def colv(v):
        return np.ascontiguousarray(v.reshape(NG, 128).T)

    return {
        "wkT": tile_w(inp["L0_attn_Wk"]), "wvT": tile_w(inp["L0_attn_Wv"]),
        "wrT": tile_w(inp["L0_attn_Wr"]), "woT": tile_w(inp["L0_attn_Wo"]),
        "w1T": tile_w(inp["L0_ffn_W1"]), "w3T": tile_w(inp["L0_ffn_W3"]),
        "w2A": np.ascontiguousarray(
            inp["L0_ffn_W2"].reshape(H, 64, 128).transpose(1, 2, 0)),
        "lnw": colv(inp["L0_attn_ln_w"]), "lnb": colv(inp["L0_attn_ln_b"]),
        "mk": colv(inp["L0_attn_mix_k"]), "mv": colv(inp["L0_attn_mix_v"]),
        "mr": colv(inp["L0_attn_mix_r"]),
        "td": colv(inp["L0_attn_time_decay"]),
        "uu": colv(inp["L0_attn_current_token_weight"]),
        "h64": _hadamard(64),
        "h128s": np.ascontiguousarray(_hadamard(128) / np.sqrt(8192.0),
                                      np.float32),
        "onesc": np.full((128, 1), 1.0 / 2048, np.float32),
        "onesr": np.ones((1, 128), np.float32),
    }


def kernel(**inp):
    x = np.ascontiguousarray(inp["L0_attn_hidden_state"])
    nb = 4
    consts = _prep_consts(inp)
    nc = build()
    in_maps = []
    for c in range(NCORE):
        if c == 0:
            xs = np.concatenate(
                [np.zeros((nb, WARM, H), np.float32), x[:, 0:TREAL]], axis=1)
        else:
            xs = x[:, c * TREAL - WARM:c * TREAL + TREAL]
        xs = np.ascontiguousarray(
            xs.reshape(nb, WARM + TREAL, NG, 128).transpose(0, 2, 3, 1))
        m = dict(consts)
        m["xs"] = xs
        m["kmask"] = np.full((128, 1), -1e9 if c == 0 else 0.0, np.float32)
        m["gmask"] = np.full((128, 1), 0.0 if c == 0 else 1.0, np.float32)
        in_maps.append(m)
    res = run_bass_kernel_spmd(nc, in_maps, core_ids=list(range(NCORE)))
    outs = []
    for c in range(NCORE):
        o = np.asarray(res.results[c]["out"])
        outs.append(np.ascontiguousarray(
            o.transpose(0, 3, 1, 2).reshape(nb, TREAL, H)))
    return np.concatenate(outs, axis=1)


